# version 03
# baseline (speedup 1.0000x reference)
import math
import numpy as np
from contextlib import ExitStack

import concourse.bass as bass
from concourse.bass import IndirectOffsetOnAxis
import concourse.mybir as mybir
from concourse.bass_utils import run_bass_kernel_spmd

F32 = mybir.dt.float32
BF16 = mybir.dt.bfloat16
I32 = mybir.dt.int32
AX = mybir.AxisListType.X
OP = mybir.AluOpType
AF = mybir.ActivationFunctionType

D = 1024
H = 16
E = 8
FF = 4096
LN16 = float(np.float32(math.log(16.0)))
EPS_ATTN = 1e-6
EPS_LN = 1e-5
SKIP_ATTN = False
PHASES = "all"
RACECHK = False
SERIAL = False

ENGINES = ("sync", "gpsimd", "tensor", "vector", "scalar")


class Builder:

    def __init__(self, nc, ctx):
        self.nc = nc
        self.ctx = ctx
        self.items = {k: [] for k in ENGINES}
        self.sem = {k: ctx.enter_context(nc.semaphore(f"sem_{k}")) for k in ENGINES}
        self.cnt = {k: 0 for k in ENGINES}
        self.waited = {k: {} for k in ENGINES}
        self.waitvals = {}
        self.csem = ctx.enter_context(nc.semaphore("sem_cc"))
        self.ccnt = 0
        self.ch_sem = {}
        self.ch_cnt = {}
        self.audit = []
        self.events = {k: [] for k in ENGINES}

    def _wait(self, eng, tok):
        peng, val = tok
        if peng == eng:
            return
        if peng == "ch:gw":
            val = self.ch_cnt[peng]
        if val > self.waited[eng].get(peng, 0):
            self.waited[eng][peng] = val
            self.audit.append((eng, peng, val))
            self.events[eng].append(("wait", peng, val))
            self.waitvals.setdefault(peng, set()).add(val)
            self.items[eng].append(("wait", peng, val))

    def _serial_waits(self, eng):
        for pe in ENGINES:
            if pe != eng and self.cnt[pe] > 0:
                self._wait(eng, (pe, self.cnt[pe]))
        for ch, cv in self.ch_cnt.items():
            if cv > 0:
                self._wait(eng, (ch, cv))

    def op(self, eng, fn, after=(), inc=1, auto=True):
        if SERIAL:
            self._serial_waits(eng)
        if eng == "tensor" and auto:
            self._wait(eng, ("vector", self.cnt["vector"]))
            self._wait(eng, ("scalar", self.cnt["scalar"]))
        for tok in after:
            if tok is not None:
                self._wait(eng, tok)
        self.cnt[eng] += 1
        self.events[eng].append(("inc", eng, 1))
        self.items[eng].append(("op", fn, self.cnt[eng]))
        return (eng, self.cnt[eng])

    def _dma_common(self, eng, ch, fn, after):
        ch = "ch:" + ch
        if ch not in self.ch_sem:
            self.ch_sem[ch] = self.ctx.enter_context(self.nc.semaphore(ch))
            self.ch_cnt[ch] = 0
        if SERIAL:
            self._serial_waits(eng)
        for tok in after:
            if tok is not None:
                self._wait(eng, tok)
        self.ch_cnt[ch] += 16
        self.events[eng].append(("inc", ch, 16))
        self.items[eng].append(("dma", fn, ch))
        return (ch, self.ch_cnt[ch])

    def dma(self, eng, ch, out_ap, in_ap, after=()):
        return self._dma_common(eng, ch,
                                lambda e, o=out_ap, i=in_ap: e.dma_start(out=o, in_=i), after)

    def dma_op(self, eng, ch, fn, after=()):
        return self._dma_common(eng, ch, fn, after)

    def collective(self, eng, ins_ap, outs_ap, after=()):
        for tok in after:
            if tok is not None:
                self._wait(eng, tok)
        self.ccnt += 1
        self.events[eng].append(("inc", "cc", 1))

        def run(e, i=ins_ap, o=outs_ap, cs=self.csem):
            e.collective_compute("AllGather", OP.bypass, replica_groups=[list(range(8))],
                                 ins=[i], outs=[o]).then_inc(cs, 1)
        self.items[eng].append(("raw", run))
        return ("cc", self.ccnt)

    def tok(self, eng):
        return (eng, self.cnt[eng])

    def finalize(self):
        ranks = {}
        for key, vals in self.waitvals.items():
            if key in ENGINES:
                if RACECHK:
                    vals = set(range(1, self.cnt[key] + 1))
                sv = sorted(vals)
                ranks[key] = {v: i + 1 for i, v in enumerate(sv)}
        if RACECHK:
            for key in ENGINES:
                ranks.setdefault(key, {v: v for v in range(1, self.cnt[key] + 1)})
        with self.nc.Block() as block:
            for name in ENGINES:
                items = self.items[name]
                eng_thresholds = ranks.get(name, {})
                if RACECHK:
                    eng_thresholds = {v: v for v in range(1, self.cnt[name] + 1)}
                sem = self.sem[name]

                def body(e, items=items, eng_thresholds=eng_thresholds, sem=sem, name=name):
                    for it in items:
                        kind = it[0]
                        if kind == "wait":
                            _, peng, val = it
                            if peng == "cc":
                                e.wait_ge(self.csem, val)
                            elif peng in ENGINES:
                                e.wait_ge(self.sem[peng], ranks[peng][val])
                            else:
                                e.wait_ge(self.ch_sem[peng], val)
                        elif kind == "op":
                            _, fn, cntv = it
                            if RACECHK and cntv > 1:
                                e.wait_ge(sem, cntv - 1)
                            inst = fn(e)
                            if cntv in eng_thresholds:
                                inst.then_inc(sem, 1)
                            if name == "vector":
                                e.engine_nop(); e.engine_nop(); e.engine_nop()
                            elif name in ("scalar", "gpsimd"):
                                e.nop()
                        elif kind == "dma":
                            _, fn, ch = it
                            fn(e).then_inc(self.ch_sem[ch], 16)
                        else:
                            it[1](e)
                getattr(block, name)(body)


def build_program(N, C_CORE, skip_attn=False):
    NS = N // 128
    NSLOT = E * C_CORE
    NCHUNK = NSLOT // 128
    CPE = C_CORE // 128
    PDEAD = NSLOT + NS
    PD2 = (PDEAD * 2 + 127) // 128
    CAPF = float(N)
    NG = C_CORE // 256
    assert C_CORE % 256 == 0 and N % 512 == 0

    nc = bass.Bass("TRN2", target_bir_lowering=False, debug=False, detect_race_conditions=RACECHK)
    PHASES_ = PHASES

    x_in = nc.dram_tensor("x", [N, D], F32, kind="ExternalInput")
    w_qkv = nc.dram_tensor("w_qkv", [8, 128, 3 * D], F32, kind="ExternalInput")
    w_out_in = nc.dram_tensor("w_out", [8, 128, D], F32, kind="ExternalInput")
    pmpair_in = nc.dram_tensor("pmpair", [128, 8, 256], F32, kind="ExternalInput")
    pmfm_in = nc.dram_tensor("pmfm", [128, H, 128], F32, kind="ExternalInput")
    wr_in = nc.dram_tensor("wr", [8, 128, E], F32, kind="ExternalInput")
    w1_in = nc.dram_tensor("w1", [E, D, FF], F32, kind="ExternalInput")
    w2_in = nc.dram_tensor("w2", [E, FF, D], F32, kind="ExternalInput")
    ident_bf_in = nc.dram_tensor("ident_bf", [128, 128], BF16, kind="ExternalInput")
    ident_f_in = nc.dram_tensor("ident_f", [128, 128], F32, kind="ExternalInput")
    iota8_in = nc.dram_tensor("iota8", [128, 8], F32, kind="ExternalInput")
    iotacol_in = nc.dram_tensor("iotacol", [128, NS], F32, kind="ExternalInput")
    hsel_in = nc.dram_tensor("hsel", [16, 2048], BF16, kind="ExternalInput")
    onesf_in = nc.dram_tensor("onesf", [128, 128], F32, kind="ExternalInput")
    ones1b_in = nc.dram_tensor("ones1b", [1, 128], BF16, kind="ExternalInput")
    rmask_in = nc.dram_tensor("rmask", [8, 1], F32, kind="ExternalInput")

    out_hbm = nc.dram_tensor("out", [N, D], F32, kind="ExternalOutput")
    DBG = PHASES.startswith("kr")
    if DBG:
        dbg_out = nc.dram_tensor("dbg", [6, 128, NS], F32, kind="ExternalOutput")
    DBGA = PHASES == "adbg"
    if DBGA or PHASES == "lndbg":
        dbga_out = nc.dram_tensor("dbga", [6, 128, 4096], F32, kind="ExternalOutput")
        scr_out = nc.dram_tensor("scr", [2, 256, D], BF16, kind="ExternalOutput")
        scr2_out = nc.dram_tensor("scr2", [4, 128, D], F32, kind="ExternalOutput")

    xn_scr = nc.dram_tensor("xn_scr", [N, D], BF16)
    q_scr = nc.dram_tensor("q_scr", [N, D], BF16)
    pairs_dram = nc.dram_tensor("pairs_dram", [128, PD2], F32)
    cc_in = nc.dram_tensor("cc_in", [8, 1], F32)
    cc_out = nc.dram_tensor("cc_out", [64, 1], F32)

    with ExitStack() as top:
        e_ = top.enter_context
        b = Builder(nc, top)

        pmpair = e_(nc.sbuf_tensor("pmpair_sb", [128, 8, 256], BF16))
        pmfm = e_(nc.sbuf_tensor("pmfm_sb", [128, H, 128], BF16))
        pmpair_n = e_(nc.sbuf_tensor("pmpairn_sb", [128, 8, 256], BF16))
        pmfm_n = e_(nc.sbuf_tensor("pmfmn_sb", [128, H, 128], BF16))
        ident_bf = e_(nc.sbuf_tensor("identbf_sb", [128, 128], BF16))
        ident_f = e_(nc.sbuf_tensor("identf_sb", [128, 128], F32))
        iota8 = e_(nc.sbuf_tensor("iota8_sb", [128, 8], F32))
        iotacol = e_(nc.sbuf_tensor("iotacol_sb", [128, NS], F32))
        hsel = e_(nc.sbuf_tensor("hsel_sb", [16, 2048], BF16))
        onesf = e_(nc.sbuf_tensor("onesf_sb", [128, 128], F32))
        ones1b = e_(nc.sbuf_tensor("ones1b_sb", [1, 128], BF16))
        rmask = e_(nc.sbuf_tensor("rmask_sb", [8, 1], F32))
        e_st = e_(nc.sbuf_tensor("e_st", [128, NS], F32))
        p_st = e_(nc.sbuf_tensor("p_st", [128, NS], F32))
        onehotT = e_(nc.sbuf_tensor("onehotT", [8, N], BF16))
        kv_sb = e_(nc.sbuf_tensor("kv_sb", [65, H, 256], F32))
        negdq_tm = e_(nc.sbuf_tensor("negdq_tm", [128, NS, 16], F32))
        ugmax = e_(nc.sbuf_tensor("ugmax", [128, 16], F32))
        gmln = e_(nc.sbuf_tensor("gmln", [16, 1], F32))
        disp_i32 = e_(nc.sbuf_tensor("disp_i32", [128, NCHUNK], I32))
        scale_sb = e_(nc.sbuf_tensor("scale_sb", [128, NCHUNK], F32))

        ps_big = e_(nc.psum_tensor("ps_big", [128, 2048], F32))
        ps_lg = e_(nc.psum_tensor("ps_lg", [128, 8], F32))
        ps_kvat = e_(nc.psum_tensor("ps_kvat", [65, 512], F32))
        ps_kv = ps_kvat[:, 0:256]
        ps_at = ps_kvat[:, 0:512]
        ps_sm = e_(nc.psum_tensor("ps_sm", [128, 512], F32))
        ps_smb = e_(nc.psum_tensor("ps_smb", [128, 1024], BF16))

        for dst, src in ((ident_bf[:, :], ident_bf_in[:, :]), (ident_f[:, :], ident_f_in[:, :]),
                         (iota8[:, :], iota8_in[:, :]), (iotacol[:, :], iotacol_in[:, :]),
                         (hsel[:, :], hsel_in[:, :]), (onesf[:, :], onesf_in[:, :]),
                         (ones1b[:, :], ones1b_in[:, :]), (rmask[:, :], rmask_in[:, :])):
            tok_consts = b.dma("sync", "c0", dst, src)
        tok_ug0 = b.op("vector", lambda e: e.memset(ugmax[:, :], -1e30))
        tok_kv0 = b.op("vector", lambda e: e.memset(kv_sb.ap(), 0.0))

        with ExitStack() as phk:
            k_ = phk.enter_context
            wqkv_sb = k_(nc.sbuf_tensor("wqkv_sb", [128, 8, 3 * D], BF16))
            wr_sb = k_(nc.sbuf_tensor("wr_sb", [128, 8, E], BF16))
            x_sub = k_(nc.sbuf_tensor("x_sub", [128, 2, D], F32))
            sq_bf = k_(nc.sbuf_tensor("sq_bf", [128, D], BF16))
            lncol = k_(nc.sbuf_tensor("lncol", [128, 8], F32))
            xn_tm = k_(nc.sbuf_tensor("xn_tm", [128, 2, D], BF16))
            xn_fm = k_(nc.sbuf_tensor("xn_fm", [128, 8, 128], BF16))
            q_tm = k_(nc.sbuf_tensor("q_tm", [128, 2, D], BF16))
            k_tm = k_(nc.sbuf_tensor("k_tm", [128, D], BF16))
            k_fm = k_(nc.sbuf_tensor("k_fm", [128, 8, 512], BF16))
            kfp = k_(nc.sbuf_tensor("kfp", [128, 4, H, 128], BF16))
            kfn = k_(nc.sbuf_tensor("kfn", [128, 4, H, 128], BF16))
            vbuf = k_(nc.sbuf_tensor("vbuf", [128, 4, H, 65], BF16))
            negdk_tm = k_(nc.sbuf_tensor("negdk_tm", [128, 16], F32))
            negdk_rows = k_(nc.sbuf_tensor("negdk_rows", [16, 512], BF16))
            lgf = k_(nc.sbuf_tensor("lgf", [128, 8], F32))
            lgtmp = k_(nc.sbuf_tensor("lgtmp", [128, 8], F32))
            lncol2 = k_(nc.sbuf_tensor("lncol2", [128, 4], F32))
            ohtile = k_(nc.sbuf_tensor("ohtile", [128, 8], BF16))
            lnr = k_(nc.sbuf_tensor("lnr", [128, 16], F32))
            dcol = k_(nc.sbuf_tensor("dcol", [128, 16], F32))

            tok_wq = b.dma("gpsimd", "gw", wqkv_sb.ap(), w_qkv.ap().rearrange("c p j -> p c j"))
            tok_wr = b.dma("gpsimd", "gw", wr_sb.ap(), wr_in.ap().rearrange("c p j -> p c j"))
            tok_pp0 = b.dma("gpsimd", "gw", pmpair.ap(), pmpair_in.ap())
            tok_pf0 = b.dma("gpsimd", "gw", pmfm.ap(), pmfm_in.ap())
            tok_pp = b.op("vector", lambda e: e.tensor_scalar(pmpair.ap(), pmpair.ap(), 0.125, None, OP.mult),
                          after=[tok_pf0])
            tok_pf = b.op("vector", lambda e: e.tensor_scalar(pmfm.ap(), pmfm.ap(), 0.125, None, OP.mult))
            tok_pp = b.op("vector", lambda e: e.tensor_scalar(pmpair_n.ap(), pmpair.ap(), -1.0, None, OP.mult))
            tok_pf = b.op("vector", lambda e: e.tensor_scalar(pmfm_n.ap(), pmfm.ap(), -1.0, None, OP.mult))
            tok_vb0 = b.op("vector", lambda e: e.memset(vbuf.ap(), 1.0), after=[tok_consts])

            tok_x = {0: b.dma("sync", "xa", x_sub[:, 0, :], x_in[0:128, :])}
            tok_xn = {}
            tok_xnst = {}
            tok_qst = {}
            tok_qev_d = {}
            tok_kv_done = tok_kv0

            for s in range(NS):
                si = s % 4
                buf = s % 2
                b.op("vector", lambda e, bu=buf: e.reduce_sum(lncol[:, 0:1], x_sub[:, bu, :], AX),
                     after=[tok_x[s]])
                b.op("vector", lambda e, bu=buf: e.tensor_mul(sq_bf[:, :], x_sub[:, bu, :], x_sub[:, bu, :]))
                b.op("vector", lambda e: e.reduce_sum(lncol[:, 1:2], sq_bf[:, :], AX))
                b.op("vector", lambda e: e.tensor_scalar(lncol[:, 2:3], lncol[:, 0:1], 1.0 / D, None, OP.mult))
                b.op("vector", lambda e: e.tensor_mul(lncol[:, 3:4], lncol[:, 2:3], lncol[:, 2:3]))
                b.op("vector", lambda e: e.tensor_scalar(lncol[:, 4:5], lncol[:, 1:2], 1.0 / D, EPS_LN, OP.mult, OP.add))
                tok_var = b.op("vector", lambda e: e.tensor_sub(lncol[:, 4:5], lncol[:, 4:5], lncol[:, 3:4]))
                b.op("scalar", lambda e: e.activation(lncol[:, 5:6], lncol[:, 4:5], AF.Ln),
                     after=[tok_var, tok_consts])
                tok_rstd = b.op("scalar", lambda e: e.activation(lncol[:, 6:7], lncol[:, 5:6], AF.Exp, scale=-0.5))
                tok_xn[s] = b.op("vector", lambda e, bu=buf: e.tensor_scalar(
                    xn_tm[:, bu, :], x_sub[:, bu, :], lncol[:, 2:3], lncol[:, 6:7], OP.subtract, OP.mult),
                    after=[tok_rstd, tok_xnst.get(s - 2)])
                tok_xnst[s] = b.dma("sync", "sna" if buf == 0 else "snb",
                                    xn_scr[s * 128:(s + 1) * 128, :], xn_tm[:, buf, :],
                                    after=[tok_xn[s]])
                if (DBGA or PHASES == "lndbg") and s == 0:
                    b.dma("gpsimd", "dbga", scr2_out.ap()[0], x_sub[:, 0, :], after=[tok_xn[0]])
                    b.dma("gpsimd", "dbga", scr2_out.ap()[1][:, 0:1024], sq_bf[:, :], after=[tok_xn[0]])
                    b.dma("gpsimd", "dbga", scr2_out.ap()[2][:, 0:8], lncol[:, :], after=[tok_xn[0]])
                    b.dma("gpsimd", "dbga", scr2_out.ap()[3][:, 0:1024], xn_tm[:, 0, :], after=[tok_xn[0]])
                    if PHASES == "lndbg":
                        b._wait("sync", ("ch:dbga", b.ch_cnt["ch:dbga"]))
                        b._wait("vector", ("ch:dbga", b.ch_cnt["ch:dbga"]))
                        b.finalize()
                        return nc
                if s + 1 < NS:
                    tok_x[s + 1] = b.dma("sync", "xa" if (s + 1) % 2 == 0 else "xb",
                                         x_sub[:, 1 - buf, :],
                                         x_in[(s + 1) * 128:(s + 2) * 128, :],
                                         after=[tok_xn.get(s - 1)])

                deps = [tok_xn[s], tok_consts]
                for c4 in range(2):
                    for cc in range(4):
                        c = c4 * 4 + cc
                        tok_tr = b.op("tensor", lambda e, bu=buf, c=c, cc=cc: e.transpose(
                            ps_smb[:, cc * 128:(cc + 1) * 128],
                            xn_tm[:, bu, c * 128:(c + 1) * 128], ident_bf[:, :]), after=deps)
                        deps = []
                    tok_fm = b.op("vector", lambda e, c4=c4: e.tensor_copy(
                        xn_fm[:, c4 * 4:(c4 + 1) * 4, :],
                        ps_smb[:, 0:512].rearrange("p (c n) -> p c n", c=4)), after=[tok_tr])
                tok_xnfm = tok_fm

                for c in range(8):
                    for nh in range(2):
                        tok_qmm = b.op("tensor", lambda e, c=c, nh=nh: e.matmul(
                            ps_big[:, nh * 512:(nh + 1) * 512], xn_fm[:, c, :],
                            wqkv_sb[:, c, nh * 512:(nh + 1) * 512],
                            start=(c == 0), stop=(c == 7)),
                            after=[tok_xnfm, tok_wq] if (c == 0 and nh == 0) else [])
                tok_qev = b.op("vector", lambda e, bu=buf: e.tensor_copy(q_tm[:, bu, :], ps_big[:, 0:1024]),
                               after=[tok_qmm, tok_qst.get(s - 2)])
                tok_qst[s] = b.dma("sync", "qsa" if buf == 0 else "qsb",
                                   q_scr[s * 128:(s + 1) * 128, :], q_tm[:, buf, :], after=[tok_qev])
                if not skip_attn:
                    b.op("vector", lambda e, bu=buf: e.tensor_mul(sq_bf[:, :], q_tm[:, bu, :], q_tm[:, bu, :]))
                    b.op("vector", lambda e: e.tensor_reduce(
                        dcol[:, :], sq_bf.ap().rearrange("p (h n) -> p h n", h=16), AX, OP.add))
                    b.op("vector", lambda e, s=s: e.tensor_scalar(
                        negdq_tm[:, s, :], dcol[:, :], -0.5, -LN16, OP.mult, OP.add))

                    for c in range(8):
                        for nh in range(2):
                            tok_kmm = b.op("tensor", lambda e, c=c, nh=nh: e.matmul(
                                ps_big[:, 1024 + nh * 512:1024 + (nh + 1) * 512], xn_fm[:, c, :],
                                wqkv_sb[:, c, D + nh * 512:D + (nh + 1) * 512],
                                start=(c == 0), stop=(c == 7)), after=[])
                    tok_kev = b.op("vector", lambda e: e.tensor_copy(k_tm[:, :], ps_big[:, 1024:2048]),
                                   after=[tok_kmm])
                    b.op("vector", lambda e: e.tensor_mul(sq_bf[:, :], k_tm[:, :], k_tm[:, :]))
                    b.op("vector", lambda e: e.tensor_reduce(
                        dcol[:, :], sq_bf.ap().rearrange("p (h n) -> p h n", h=16), AX, OP.add))
                    tok_ndk = b.op("vector", lambda e: e.tensor_scalar(
                        negdk_tm[:, :], dcol[:, :], -0.5, -LN16, OP.mult, OP.add))
                    tok_ndt = b.op("tensor", lambda e: e.transpose(
                        ps_sm[0:16, 0:128], negdk_tm[:, :], ident_f[:, :]), after=[tok_ndk])
                    tok_ndr = b.op("vector", lambda e, si=si: e.tensor_copy(
                        negdk_rows[:, si * 128:(si + 1) * 128], ps_sm[0:16, 0:128]), after=[tok_ndt])
                    for c4 in range(2):
                        for cc in range(4):
                            c = c4 * 4 + cc
                            tok_tr2 = b.op("tensor", lambda e, c=c, cc=cc: e.transpose(
                                ps_smb[:, cc * 128:(cc + 1) * 128],
                                k_tm[:, c * 128:(c + 1) * 128], ident_bf[:, :]), after=[])
                        tok_kfm = b.op("vector", lambda e, c4=c4, si=si: e.tensor_copy(
                            k_fm[:, c4 * 4:(c4 + 1) * 4, si * 128:(si + 1) * 128],
                            ps_smb[:, 0:512].rearrange("p (c n) -> p c n", c=4)), after=[tok_tr2])
                    tok_kfm_done = tok_kfm

                for c in range(8):
                    for nh in range(2):
                        tok_vmm = b.op("tensor", lambda e, c=c, nh=nh: e.matmul(
                            ps_big[:, nh * 512:(nh + 1) * 512], xn_fm[:, c, :],
                            wqkv_sb[:, c, 2 * D + nh * 512:2 * D + (nh + 1) * 512],
                            start=(c == 0), stop=(c == 7)), after=[tok_wr] if (c == 0 and nh == 0) else [])
                    tok_lmm = b.op("tensor", lambda e, c=c: e.matmul(
                        ps_lg[:, :], xn_fm[:, c, :], wr_sb[:, c, :],
                        start=(c == 0), stop=(c == 7)))
                if not skip_attn:
                    tok_vev = b.op("vector", lambda e, si=si: e.tensor_copy(
                        vbuf[:, si, :, 0:64],
                        ps_big[:, 0:1024].rearrange("p (h n) -> p h n", h=16)),
                        after=[tok_vmm, tok_vb0])

                b.op("vector", lambda e: e.tensor_copy(lgf[:, :], ps_lg[:, :]), after=[tok_lmm])
                b.op("vector", lambda e: e.reduce_max(lncol2[:, 3:4], lgf[:, :], AX))
                tok_nm = b.op("vector", lambda e: e.tensor_scalar(lncol2[:, 0:1], lncol2[:, 3:4], -1.0, None, OP.mult))
                tok_ex = b.op("scalar", lambda e: e.activation(lgtmp[:, :], lgf[:, :], AF.Exp, bias=lncol2[:, 0:1]),
                              after=[tok_nm])
                tok_sm = b.op("vector", lambda e: e.reduce_sum(lncol2[:, 1:2], lgtmp[:, :], AX), after=[tok_ex])
                b.op("vector", lambda e, s=s: e.reciprocal(p_st[:, s:s + 1], lncol2[:, 1:2]))
                b.op("vector", lambda e: e.tensor_single_scalar(lgtmp[:, :], lgf[:, :], lncol2[:, 3:4], OP.is_lt))
                b.op("vector", lambda e: e.scalar_tensor_tensor(lgtmp[:, :], lgtmp[:, :], 1000.0, iota8[:, :], OP.mult, OP.add))
                tok_e = b.op("vector", lambda e, s=s: e.tensor_reduce(e_st[:, s:s + 1], lgtmp[:, :], AX, OP.min))
                tok_oh = b.op("vector", lambda e, s=s: e.tensor_single_scalar(
                    ohtile[:, :], iota8[:, :], e_st[:, s:s + 1], OP.is_equal))
                tok_oht = b.op("tensor", lambda e: e.transpose(ps_smb[0:8, 0:128], ohtile[:, :], ident_bf[:, :]),
                               after=[tok_oh])
                b.op("vector", lambda e, s=s: e.tensor_copy(
                    onehotT[:, s * 128:(s + 1) * 128], ps_smb[0:8, 0:128]), after=[tok_oht])

                if not skip_attn:
                    tok_kf2 = None
                    for half in range(2):
                        hs = hsel[:, half * 1024:(half + 1) * 1024]
                        for pn in range(2):
                            for nh in range(2):
                                tok_h2 = b.op("tensor", lambda e, si=si, nh=nh, pn=pn, hs=hs: e.matmul(
                                    ps_big[:, pn * 1024 + nh * 512:pn * 1024 + (nh + 1) * 512],
                                    negdk_rows[:, si * 128:(si + 1) * 128], hs[:, nh * 512:(nh + 1) * 512],
                                    start=True, stop=False),
                                    after=[tok_vev, tok_kfm_done, tok_pp, tok_ndr, tok_kf2]
                                    if (nh == 0 and pn == 0) else [])
                        for pn, pmx in ((0, pmpair), (1, pmpair_n)):
                            for cc in range(4):
                                c = half * 4 + cc
                                tok_h2 = b.op("tensor", lambda e, c=c, cc=cc, si=si, pn=pn, pmx=pmx: e.matmul(
                                    ps_big[:, pn * 1024 + cc * 256:pn * 1024 + (cc + 1) * 256],
                                    k_fm[:, c, si * 128:(si + 1) * 128], pmx[:, c, :],
                                    start=False, stop=(cc % 2 == 1)))
                        tok_kf1 = b.op("scalar", lambda e, si=si, half=half: e.activation(
                            kfp[:, si, half * 8:(half + 1) * 8, :],
                            ps_big[:, 0:1024].rearrange("p (h n) -> p h n", h=8), AF.Exp),
                            after=[tok_h2])
                        tok_kf2 = b.op("scalar", lambda e, si=si, half=half: e.activation(
                            kfn[:, si, half * 8:(half + 1) * 8, :],
                            ps_big[:, 1024:2048].rearrange("p (h n) -> p h n", h=8), AF.Exp))
                    tok_rm = b.op("vector", lambda e, si=si: e.tensor_reduce(
                        lnr[:, :], kfp[:, si, :, :], AX, OP.max), after=[tok_kf1])
                    tok_ln = b.op("scalar", lambda e: e.activation(lnr[:, :], lnr[:, :], AF.Ln), after=[tok_rm])
                    b.op("vector", lambda e: e.tensor_sub(lnr[:, :], lnr[:, :], negdk_tm[:, :]),
                         after=[tok_ln])
                    b.op("vector", lambda e: e.tensor_max(ugmax[:, :], ugmax[:, :], lnr[:, :]),
                         after=[tok_ug0])

                    if si == 3:
                        tok_prev = None
                        for h in range(H):
                            for si2 in range(4):
                                b.op("tensor", lambda e, h=h, si2=si2: e.matmul(
                                    ps_kv[:, 0:128], vbuf[:, si2, h, :], kfp[:, si2, h, :],
                                    start=(si2 == 0), stop=False),
                                    after=[tok_kf1, tok_kf2, tok_kv_done] if (h == 0 and si2 == 0) else [])
                                tok_kvm = b.op("tensor", lambda e, h=h, si2=si2: e.matmul(
                                    ps_kv[:, 128:256], vbuf[:, si2, h, :], kfn[:, si2, h, :],
                                    start=False, stop=(si2 == 3)))
                            tok_prev = b.op("vector", lambda e, h=h: e.tensor_add(
                                kv_sb[:, h, :], kv_sb[:, h, :], ps_kv[:, :]), after=[tok_kvm])
                        tok_kv_done = tok_prev

            tok_phaseK_v = b.tok("vector")
            tok_st_all = [tok_xnst[NS - 1], tok_xnst[NS - 2], tok_qst[NS - 1], tok_qst[NS - 2]]
            tok_xnst_all = tok_xnst[NS - 1]
            tok_qst_all = tok_qst[NS - 1]
            if PHASES == "k":
                b._wait("sync", tok_phaseK_v)
                b._wait("gpsimd", tok_phaseK_v)
                b.finalize()
                return nc

            zrow8 = k_(nc.sbuf_tensor("zrow8", [8, 512], BF16))
            rkT = k_(nc.sbuf_tensor("rkT", [8, 512], F32))
            rkEx = k_(nc.sbuf_tensor("rkEx", [8, 2, 512], F32))
            carry8 = k_(nc.sbuf_tensor("carry8", [8, 1], F32))
            rk_dram = nc.dram_tensor("rk_dram", [8, N], F32)
            tok_z8 = b.op("vector", lambda e: e.memset(zrow8[:, :], 0.0))
            b.op("vector", lambda e: e.memset(carry8[:, :], 0.0))
            NCH = N // 512
            tok_spill = [None] * NCH
            for ch in range(NCH):
                tok_scan = b.op("vector", lambda e, ch=ch: e.tensor_tensor_scan(
                    rkT[:, :], onehotT[:, ch * 512:(ch + 1) * 512],
                    zrow8[:, :], 0.0, OP.add, OP.add),
                    after=[tok_phaseK_v, tok_z8] if ch == 0 else [])
                if ch > 0:
                    tok_scan = b.op("vector", lambda e, ch=ch: e.tensor_scalar_add(
                        rkT[:, :], rkT[:, :], carry8[:, 0:1]))
                if ch + 1 < NCH:
                    b.op("vector", lambda e, ch=ch: e.tensor_copy(carry8[:, :], rkT[:, 511:512]))
                par = ch % 2
                sub_deps = [tok_spill[ch - 2]] if ch >= 2 else []
                tok_sub = b.op("vector", lambda e, ch=ch, par=par: e.tensor_sub(
                    rkEx[:, par, :], rkT[:, :], onehotT[:, ch * 512:(ch + 1) * 512]),
                    after=sub_deps)
                tok_spill[ch] = b.dma("sync", "rs" + "ab"[par],
                                      rk_dram.ap()[:, ch * 512:(ch + 1) * 512],
                                      rkEx[:, par, :], after=[tok_sub])
            cnt_sb = k_(nc.sbuf_tensor("cnt_sb", [8, 1], F32))
            tok_cnt = b.op("vector", lambda e: e.tensor_copy(cnt_sb[:, :], rkT[:, 511:512]), after=[tok_scan])
            if PHASES == "kr1":
                b._wait("sync", tok_cnt)
                b._wait("gpsimd", tok_cnt)
                b.finalize()
                return nc
            tok_ccin = b.dma("gpsimd", "cb", cc_in.ap(), cnt_sb[:, :], after=[tok_cnt])
            tok_cc = b.collective("gpsimd", cc_in.ap(), cc_out.ap(), after=[tok_ccin])
            allc_sb = k_(nc.sbuf_tensor("allc_sb", [8, 8], F32))
            tok_allc = b.dma("gpsimd", "cb", allc_sb[:, :],
                             cc_out.ap().rearrange("(r e) one -> r (e one)", r=8), after=[tok_cc])
            offrow = k_(nc.sbuf_tensor("offrow", [1, 8], F32))
            offb = k_(nc.sbuf_tensor("offb", [128, 8], F32))
            tok_off = b.op("tensor", lambda e: e.matmul(ps_lg[0:1, 0:8], rmask[:, :], allc_sb[:, :],
                                                        start=True, stop=True), after=[tok_allc])
            tok_offr = b.op("vector", lambda e: e.tensor_copy(offrow[:, :], ps_lg[0:1, 0:8]), after=[tok_off])
            tok_offb_mm = b.op("tensor", lambda e: e.matmul(ps_lg[:, :], onesf[0:1, :], offrow[:, :],
                                                            start=True, stop=True), after=[tok_offr])
            tok_offb = b.op("vector", lambda e: e.tensor_copy(offb[:, :], ps_lg[:, :]), after=[tok_offb_mm])
            if PHASES == "kr2":
                b._wait("sync", tok_offb)
                b._wait("gpsimd", tok_offb)
                b.finalize()
                return nc

            lrank = k_(nc.sbuf_tensor("lrank", [128, NS], F32))
            grank = k_(nc.sbuf_tensor("grank", [128, NS], F32))
            rtmp = k_(nc.sbuf_tensor("rtmp", [128, 8], F32))
            rtmp2 = k_(nc.sbuf_tensor("rtmp2", [128, 8], F32))
            ohre = k_(nc.sbuf_tensor("ohre", [128, 8], F32))
            rkLd = k_(nc.sbuf_tensor("rkLd", [8, 2, 512], F32))
            tok_load = [None] * NCH
            tok_load[0] = b.dma("gpsimd", "rla", rkLd[:, 0, :],
                                rk_dram.ap()[:, 0:512], after=[tok_spill[0]])
            if NCH > 1:
                tok_load[1] = b.dma("gpsimd", "rlb", rkLd[:, 1, :],
                                    rk_dram.ap()[:, 512:1024], after=[tok_spill[1]])
            for s in range(NS):
                g = s // 4
                tok_rt = b.op("tensor", lambda e, s=s, g=g: e.transpose(
                    ps_sm[0:128, 0:8], rkLd[:, g % 2, (s % 4) * 128:(s % 4) * 128 + 128],
                    ident_f[0:8, 0:8]),
                    after=[tok_load[g], tok_offb] if s % 4 == 0 else [])
                if s % 4 == 3 and g + 2 < NCH:
                    tok_load[g + 2] = b.dma("gpsimd", "rl" + "ab"[g % 2],
                                            rkLd[:, g % 2, :],
                                            rk_dram.ap()[:, (g + 2) * 512:(g + 3) * 512],
                                            after=[tok_spill[g + 2], tok_rt])
                b.op("vector", lambda e, s=s: e.tensor_single_scalar(
                    ohre[:, :], iota8[:, :], e_st[:, s:s + 1], OP.is_equal), after=[tok_rt])
                b.op("vector", lambda e: e.tensor_mul(rtmp[:, :], ps_sm[0:128, 0:8], ohre[:, :]))
                b.op("vector", lambda e, s=s: e.reduce_sum(lrank[:, s:s + 1], rtmp[:, :], AX))
                b.op("vector", lambda e: e.tensor_mul(rtmp2[:, :], offb[:, :], ohre[:, :]))
                b.op("vector", lambda e, s=s: e.reduce_sum(grank[:, s:s + 1], rtmp2[:, :], AX))
            tok_gr = b.op("vector", lambda e: e.tensor_add(grank[:, :], grank[:, :], lrank[:, :]))
            keepm = k_(nc.sbuf_tensor("keepm", [128, NS], F32))
            b.op("vector", lambda e: e.tensor_single_scalar(keepm[:, :], grank[:, :], CAPF, OP.is_lt),
                 after=[tok_gr])
            tok_scl = b.op("vector", lambda e: e.tensor_mul(keepm[:, :], keepm[:, :], p_st[:, :]))
            if DBG:
                for i_, tns in enumerate((e_st, p_st, lrank, grank, keepm)):
                    b.dma("gpsimd", "dbgch", dbg_out.ap()[i_], tns[:, :], after=[tok_scl])
            if PHASES == "kr3":
                b._wait("sync", tok_scl)
                b._wait("gpsimd", ("ch:dbgch", b.ch_cnt["ch:dbgch"]))
                b.finalize()
                return nc
            ph_a = k_(nc.sbuf_tensor("ph_a", [128, NS], F32))
            ph_q = k_(nc.sbuf_tensor("ph_q", [128, NS], F32))
            ph_i = k_(nc.sbuf_tensor("ph_i", [128, NS], I32))
            ph_r = k_(nc.sbuf_tensor("ph_r", [128, NS], F32))
            b.op("vector", lambda e: e.tensor_scalar(ph_a[:, :], lrank[:, :], 1.0 / 128.0, None, OP.mult))
            b.op("vector", lambda e: e.tensor_copy(ph_i[:, :], ph_a[:, :]))
            b.op("vector", lambda e: e.tensor_copy(ph_q[:, :], ph_i[:, :]))
            b.op("vector", lambda e: e.scalar_tensor_tensor(ph_r[:, :], ph_q[:, :], -128.0, lrank[:, :], OP.mult, OP.add))
            b.op("vector", lambda e: e.tensor_single_scalar(ph_a[:, :], ph_r[:, :], 0.0, OP.is_lt))
            b.op("vector", lambda e: e.tensor_sub(ph_q[:, :], ph_q[:, :], ph_a[:, :]))
            b.op("vector", lambda e: e.scalar_tensor_tensor(ph_r[:, :], ph_a[:, :], 128.0, ph_r[:, :], OP.mult, OP.add))
            b.op("vector", lambda e: e.scalar_tensor_tensor(ph_q[:, :], e_st[:, :], float(CPE), ph_q[:, :], OP.mult, OP.add))
            b.op("vector", lambda e: e.scalar_tensor_tensor(ph_q[:, :], ph_r[:, :], float(NCHUNK), ph_q[:, :], OP.mult, OP.add))
            b.op("vector", lambda e: e.tensor_single_scalar(ph_a[:, :], lrank[:, :], float(C_CORE) - 0.5, OP.is_gt))
            b.op("vector", lambda e: e.tensor_scalar(ph_r[:, :], iotacol[:, :], 1.0, float(NSLOT), OP.mult, OP.add))
            b.op("vector", lambda e: e.tensor_sub(ph_r[:, :], ph_r[:, :], ph_q[:, :]))
            b.op("vector", lambda e: e.tensor_mul(ph_r[:, :], ph_r[:, :], ph_a[:, :]))
            b.op("vector", lambda e: e.tensor_add(ph_q[:, :], ph_q[:, :], ph_r[:, :]))
            b.op("vector", lambda e: e.tensor_scalar_max(ph_q[:, :], ph_q[:, :], 0.0))
            b.op("vector", lambda e: e.tensor_scalar_min(ph_q[:, :], ph_q[:, :], float(PDEAD - 1)))
            if DBG:
                b.dma("gpsimd", "dbgch", dbg_out.ap()[5], ph_q[:, :], after=[(
                    "vector", b.cnt["vector"])])
            tok_phi = b.op("vector", lambda e: e.tensor_copy(ph_i[:, :], ph_q[:, :]))
            prs = k_(nc.sbuf_tensor("prs", [128, NS, 2], F32))
            ids_f = k_(nc.sbuf_tensor("ids_f", [128, NS], F32))
            tok_iota = b.op("gpsimd", lambda e: e.iota(ids_f[:, :], [[128, NS]], channel_multiplier=1,
                                                       allow_small_or_imprecise_dtypes=True),
                            after=[tok_allc])
            tok_p0 = b.op("vector", lambda e: e.tensor_copy(prs[:, :, 0], ids_f[:, :]), after=[tok_iota])
            tok_p1 = b.op("vector", lambda e: e.tensor_copy(prs[:, :, 1], keepm[:, :]), after=[tok_scl])
            zpair = k_(nc.sbuf_tensor("zpair", [128, PD2], F32))
            tok_zp = b.op("vector", lambda e: e.memset(zpair[:, :], 0.0))
            tok_zpd = b.dma("sync", "pz", pairs_dram.ap(), zpair[:, :], after=[tok_zp])
            if PHASES == "kr4":
                b._wait("sync", tok_p0)
                b._wait("gpsimd", tok_zpd)
                b._wait("gpsimd", tok_p1)
                b.finalize()
                return nc
            pairs_flat = pairs_dram.ap().rearrange("p k -> (p k)")
            tok_sc_last = None
            for s in range(NS):
                deps = [tok_phi, tok_p0, tok_p1, tok_zpd] if s == 0 else []
                tok_sc_last = b.dma_op("gpsimd", "ps", lambda e, s=s: e.indirect_dma_start(
                    pairs_flat.rearrange("(r k) -> r k", k=2), IndirectOffsetOnAxis(ph_i[:, s:s + 1], 0),
                    prs[:, s, :], None), after=deps)
            prs2 = k_(nc.sbuf_tensor("prs2", [128, NCHUNK, 2], F32))
            tok_ld2 = b.dma("sync", "pl", prs2.ap(),
                            pairs_flat[0:NSLOT * 2].rearrange("(p c k) -> p c k", p=128, k=2),
                            after=[tok_sc_last])
            tok_disp = b.op("vector", lambda e: e.tensor_copy(disp_i32[:, :], prs2[:, :, 0]), after=[tok_ld2])
            tok_scale = b.op("vector", lambda e: e.tensor_copy(scale_sb[:, :], prs2[:, :, 1]))
            if PHASES == "kr5":
                b._wait("sync", tok_scale)
                b._wait("gpsimd", tok_scale)
                b.finalize()
                return nc

            if not skip_attn:
                tok_gt = b.op("tensor", lambda e: e.transpose(ps_sm[0:16, 0:128], ugmax[:, :], ident_f[:, :]),
                              after=[])
                tok_gmr = b.op("vector", lambda e: e.tensor_reduce(gmln[:, :], ps_sm[0:16, 0:128], AX, OP.max),
                               after=[tok_gt])
            else:
                tok_gmr = None
            tok_routing_g = b.tok("gpsimd")
            tok_routing_v = b.tok("vector")
            if PHASES == "kr":
                b._wait("sync", tok_scale)
                b._wait("gpsimd", tok_scale)
                b.finalize()
                return nc
            if PHASES == "kra":
                with ExitStack() as tmpq:
                    wout_t = tmpq.enter_context(nc.sbuf_tensor("wout_sb", [128, 8, D], BF16))
                    tok_wo_t = b.dma("gpsimd", "gw", wout_t.ap(), w_out_in.ap().rearrange("c p j -> p c j"),
                                     after=[tok_scale] + tok_st_all + ([tok_gmr] if tok_gmr else []))
                    b._wait("sync", tok_wo_t)
                    b._wait("vector", tok_wo_t)
                    b.finalize()
                return nc

        with ExitStack() as phq:
            q_ = phq.enter_context
            wout_sb = q_(nc.sbuf_tensor("wout_sb", [128, 8, D], BF16))
            tok_wo = b.dma("gpsimd", "gw", wout_sb.ap(), w_out_in.ap().rearrange("c p j -> p c j"), after=[tok_scale, tok_gmr] + tok_st_all)
            tok_outst = {}

            qa_stack = ExitStack()
            qa = qa_stack.enter_context
            if not skip_attn:
                kvaug = q_(nc.sbuf_tensor("kvaug_sb", [128, H, 2, 65], BF16))
                q_sub = qa(nc.sbuf_tensor("q_sub", [128, 2, D], BF16))
                q_fm = qa(nc.sbuf_tensor("q_fm", [128, 8, 512], BF16))
                x2_sub = qa(nc.sbuf_tensor("x2_sub", [128, 4, D], F32))
                stab = qa(nc.sbuf_tensor("stab", [128, 16], F32))
                negs16 = qa(nc.sbuf_tensor("negs16", [16, 128], F32))
                negs_p0 = qa(nc.sbuf_tensor("negs_p0", [1, 16, 2, 512], BF16))
                qfa = qa(nc.sbuf_tensor("qfa", [128, 512], BF16))
                qfb = qa(nc.sbuf_tensor("qfb", [128, 512], BF16))
                zr = qa(nc.sbuf_tensor("zr", [128, 512], F32))
                rb_sb = qa(nc.sbuf_tensor("rb_sb", [64, 512], F32))
                apart = qa(nc.sbuf_tensor("apart", [64, 512], BF16))
                attnbuf = qa(nc.sbuf_tensor("attnbuf", [128, 8, 512], BF16))
                out_sb = qa(nc.sbuf_tensor("out_sb", [128, 2, D], F32))

                tok_kvaug = None
                for h in range(H):
                    for j in range(2):
                        tok_tr = b.op("tensor", lambda e, h=h, j=j: e.transpose(
                            ps_sm[0:128, 0:65], kv_sb[:, h, j * 128:(j + 1) * 128], ident_f[0:65, 0:65]),
                            after=([tok_kv_done, tok_scale, tok_gmr] + tok_st_all) if (h == 0 and j == 0) else [])
                        tok_kvaug = b.op("vector", lambda e, h=h, j=j: e.tensor_copy(
                            kvaug[:, h, j, :], ps_sm[0:128, 0:65]), after=[tok_tr])

                tok_qld = {0: b.dma("sync", "qla", q_sub[:, 0, :], q_scr[0:128, :],
                                     after=[tok_scale, tok_gmr] + tok_st_all)}
                tok_xld = {}
                prev_qfm = {}
                tok_nsd = {}
                tok_oadd_last = None
                for s in range(NS):
                    t, si = s // 4, s % 4
                    buf = s % 2
                    if si == 0:
                        for si2 in range(4):
                            tok_xld[t] = b.dma("sync", "x2", x2_sub[:, si2, :],
                                               x_in[(t * 4 + si2) * 128:(t * 4 + si2 + 1) * 128, :],
                                               after=([tok_oadd_last] if t > 0 else [tok_scale, tok_gmr] + tok_st_all) if si2 == 0 else [])
                    deps = [tok_qld[s], tok_kvaug if s == 0 else None]
                    for c4 in range(2):
                        for cc in range(4):
                            c = c4 * 4 + cc
                            tok_tr = b.op("tensor", lambda e, bu=buf, c=c, cc=cc: e.transpose(
                                ps_smb[:, cc * 128:(cc + 1) * 128],
                                q_sub[:, bu, c * 128:(c + 1) * 128], ident_bf[:, :]), after=deps)
                            deps = []
                        tok_ev = b.op("vector", lambda e, c4=c4, si=si: e.tensor_copy(
                            q_fm[:, c4 * 4:(c4 + 1) * 4, si * 128:(si + 1) * 128],
                            ps_smb[:, 0:512].rearrange("p (c n) -> p c n", c=4)), after=[tok_tr])
                    prev_qfm[s] = tok_ev
                    if s + 1 < NS:
                        tok_qld[s + 1] = b.dma("sync", "qla" if (s + 1) % 2 == 0 else "qlb",
                                               q_sub[:, 1 - buf, :],
                                               q_scr[(s + 1) * 128:(s + 2) * 128, :],
                                               after=[prev_qfm.get(s - 1)])
                    for c in range(8):
                        tok_umm = b.op("tensor", lambda e, c=c, si=si: e.matmul(
                            ps_big[:, c * 256:(c + 1) * 256],
                            q_fm[:, c, si * 128:(si + 1) * 128], pmpair[:, c, :],
                            start=True, stop=True), after=[tok_ev, tok_pp] if c == 0 else [])
                    tok_stab = b.op("vector", lambda e: e.tensor_reduce(
                        stab[:, :], ps_big.ap().rearrange("p (h n) -> p h n", h=16), AX, OP.max),
                        after=[tok_umm])
                    tok_ns = b.op("vector", lambda e, s=s: e.tensor_sub(stab[:, :], negdq_tm[:, s, :], stab[:, :]),
                                  after=[tok_stab])
                    tok_nst = b.op("tensor", lambda e: e.transpose(ps_sm[0:16, 0:128], stab[:, :], ident_f[:, :]),
                                   after=[tok_ns])
                    tok_n16 = b.op("vector", lambda e: e.tensor_scalar(
                        negs16[:, :], ps_sm[0:16, 0:128], gmln[:, 0:1], None, OP.subtract),
                        after=[tok_nst, tok_gmr, tok_nsd.get(s - 1)])
                    tok_nsd[s] = b.dma("gpsimd", "nga" if t % 2 == 0 else "ngb",
                                       negs_p0.ap()[0:1, :, t % 2, si * 128:(si + 1) * 128],
                                       negs16[:, :], after=[tok_n16])

                    if si == 3:
                        tok_ab = None
                        tok_ap_prev = None
                        for h in range(H):
                            b.op("tensor", lambda e, h=h, t=t: e.matmul(
                                ps_big[:, 0:512], ones1b[:, :], negs_p0[0:1, h, t % 2, :],
                                start=True, stop=False),
                                after=[prev_qfm[s], tok_nsd[s], tok_pf] if h == 0 else [])
                            b.op("tensor", lambda e, h=h, t=t: e.matmul(
                                ps_big[:, 512:1024], ones1b[:, :], negs_p0[0:1, h, t % 2, :],
                                start=True, stop=False))
                            b.op("tensor", lambda e, h=h: e.matmul(
                                ps_big[:, 0:512], pmfm[:, h, :], q_fm[:, h // 2, :],
                                start=False, stop=True))
                            tok_bc = b.op("tensor", lambda e, h=h: e.matmul(
                                ps_big[:, 512:1024], pmfm_n[:, h, :], q_fm[:, h // 2, :],
                                start=False, stop=True))
                            tok_qf1 = b.op("scalar", lambda e: e.activation(qfa[:, :], ps_big[:, 0:512], AF.Exp),
                                           after=[tok_bc])
                            tok_qf2 = b.op("scalar", lambda e: e.activation(qfb[:, :], ps_big[:, 512:1024], AF.Exp))
                            tok_am1 = b.op("tensor", lambda e, h=h: e.matmul(
                                ps_at[:, :], kvaug[:, h, 0, :], qfa[:, :], start=True, stop=False),
                                after=[tok_qf1, tok_qf2])
                            tok_am2 = b.op("tensor", lambda e, h=h: e.matmul(
                                ps_at[:, :], kvaug[:, h, 1, :], qfb[:, :], start=False, stop=True))
                            tok_z = b.op("vector", lambda e: e.tensor_scalar(
                                zr[64:65, :], ps_at[64:65, :], 1.0, EPS_ATTN, OP.mult, OP.add),
                                after=[tok_am2])
                            tok_r = b.op("vector", lambda e: e.reciprocal(zr[64:65, :], zr[64:65, :]))
                            tok_rb = b.op("tensor", lambda e: e.matmul(
                                ps_sm[0:64, 0:512], onesf[64:65, 0:64], zr[64:65, :], start=True, stop=True),
                                after=[tok_r])
                            tok_rbe = b.op("vector", lambda e: e.tensor_copy(rb_sb[:, :], ps_sm[0:64, 0:512]),
                                           after=[tok_rb])
                            tok_ap = b.op("vector", lambda e: e.scalar_tensor_tensor(
                                apart[:, :], ps_at[0:64, :], 1.0, rb_sb[:, :], OP.mult, OP.mult),
                                after=[tok_rbe])
                            tok_ap_prev = tok_ap
                            if h % 2 == 0:
                                tok_ab = b.op("vector", lambda e, h=h: e.tensor_copy(
                                    attnbuf[0:64, h // 2, :], apart[:, :]), after=[tok_ap])
                            else:
                                tok_sh = b.op("tensor", lambda e: e.matmul(
                                    ps_big[64:128, 1024:1536], ident_bf[0:64, 0:64], apart[:, :],
                                    start=True, stop=True), after=[tok_ap])
                                tok_ab = b.op("vector", lambda e, h=h: e.tensor_copy(
                                    attnbuf[64:128, h // 2, :], ps_big[64:128, 1024:1536]), after=[tok_sh])
                        for si2 in range(4):
                            s2 = t * 4 + si2
                            bu2 = s2 % 2
                            for c in range(8):
                                for nh in range(2):
                                    tok_wm = b.op("tensor", lambda e, c=c, si2=si2, nh=nh: e.matmul(
                                        ps_big[:, nh * 512:(nh + 1) * 512],
                                        attnbuf[:, c, si2 * 128:(si2 + 1) * 128],
                                        wout_sb[:, c, nh * 512:(nh + 1) * 512],
                                        start=(c == 0), stop=(c == 7)),
                                        after=[tok_ab, tok_wo] if (c == 0 and nh == 0) else [])
                            tok_oadd = b.op("vector", lambda e, bu2=bu2, si2=si2: e.tensor_add(
                                out_sb[:, bu2, :], x2_sub[:, si2, :], ps_big[:, 0:1024]),
                                after=[tok_wm, tok_xld[t], tok_outst.get(s2 - 2)])
                            tok_outst[s2] = b.dma("sync", "oa" if bu2 == 0 else "ob",
                                                  out_hbm[s2 * 128:(s2 + 1) * 128, :],
                                                  out_sb[:, bu2, :], after=[tok_oadd])
                            tok_oadd_last = tok_oadd
                tok_base_done = tok_outst[NS - 1]
                tok_base_list = [tok_outst[NS - 1], tok_outst[NS - 2]]
            else:
                x2_sub = qa(nc.sbuf_tensor("x2_sub", [128, 2, D], F32))
                tok_xld = {0: b.dma("sync", "x2a", x2_sub[:, 0, :], x_in[0:128, :], after=[tok_scale] + tok_st_all + ([tok_gmr] if tok_gmr else []))}
                tok_cp = {}
                for s in range(NS):
                    buf = s % 2
                    if s + 1 < NS:
                        tok_xld[s + 1] = b.dma("sync", "x2a" if (s + 1) % 2 == 0 else "x2b",
                                               x_sub[:, 1 - buf, :] if False else x2_sub[:, 1 - buf, :],
                                               x_in[(s + 1) * 128:(s + 2) * 128, :],
                                               after=[tok_cp.get(s - 1)])
                    if PHASES == "krq0":
                        tok_cp[s] = b.dma("gpsimd", "oc", out_hbm[0:128, :], x_in[0:128, :],
                                          after=[tok_xld[s]])
                    else:
                        tok_cp[s] = b.dma("gpsimd", "oc", out_hbm[s * 128:(s + 1) * 128, :], x2_sub[:, buf, :],
                                          after=[tok_xld[s]])
                tok_base_done = tok_cp[NS - 1]
                tok_base_list = [tok_base_done]

            if DBGA and not skip_attn:
                vtok = ("vector", b.cnt["vector"])
                b.dma("gpsimd", "dbga", scr_out.ap()[0], xn_scr[0:256, :], after=[vtok])
                b.dma("gpsimd", "dbga", scr_out.ap()[1], q_scr[0:256, :], after=[vtok])
                b.dma("gpsimd", "dbga", dbga_out.ap()[0][0:65, 0:4096], kv_sb.ap(), after=[vtok])
                b.dma("gpsimd", "dbga", dbga_out.ap()[1][:, 0:16], ugmax[:, :], after=[vtok])
                b.dma_op("gpsimd", "dbga", lambda e: e.dma_start(out=dbga_out.ap()[1][0:16, 16:17], in_=gmln[:, :], allow_slow_non_contiguous=True), after=[vtok])
                for hh in range(4):
                    b.dma("gpsimd", "dbga", dbga_out.ap()[1][0:1, 2048 + hh * 512:2048 + (hh + 1) * 512], negs_p0.ap()[0:1, hh, 0, :], after=[vtok])
                b.dma("gpsimd", "dbga", dbga_out.ap()[2][:, 0:512], qfa[:, :], after=[vtok])
                b.dma("gpsimd", "dbga", dbga_out.ap()[2][:, 512:1024], qfb[:, :], after=[vtok])
                b.dma("gpsimd", "dbga", dbga_out.ap()[3], attnbuf.ap(), after=[vtok])
                b.dma("gpsimd", "dbga", dbga_out.ap()[4][:, 0:NS * 16], negdq_tm.ap(), after=[vtok])
                b.dma("gpsimd", "dbga", dbga_out.ap()[5][:, 0:2080], kvaug.ap().rearrange("p h t n -> p (h t n)"), after=[vtok])
                tok_dbga = ("ch:dbga", b.ch_cnt["ch:dbga"])
                b._wait("sync", tok_dbga)
            if PHASES in ("krq", "krq0"):
                for chx in ("ch:oa", "ch:ob", "ch:oc"):
                    if chx in b.ch_cnt:
                        for eng in ("sync", "gpsimd", "vector"):
                            b._wait(eng, (chx, b.ch_cnt[chx]))
                qa_stack.close()
                b.finalize()
                return nc
            qa_stack.close()
            w1s = q_(nc.sbuf_tensor("w1s", [128, 2, 8, 512], BF16))
            w2s = q_(nc.sbuf_tensor("w2s", [128, 32, D], BF16))
            xe_tm = q_(nc.sbuf_tensor("xe_tm", [128, 2, D], BF16))
            xe_fm = q_(nc.sbuf_tensor("xe_fm", [128, 8, 256], BF16))
            h_buf = q_(nc.sbuf_tensor("h_buf", [128, 32, 256], BF16))
            y_stage = q_(nc.sbuf_tensor("y_stage", [128, 2, D], F32))

            tok_w1 = {}
            tok_hm_last = {}
            tok_w2 = {}
            tok_h_done = {}
            tok_y_done = {}
            tok_fm_prev = None
            tok_ysc = None

            for ex in range(E):
                tok_w2[ex] = b.dma("gpsimd", "w2", w2s.ap(),
                                   w2_in.ap()[ex].rearrange("(c p) d -> p c d", p=128),
                                   after=[tok_y_done.get(ex - 1)] + (tok_base_list if ex == 0 else []))
                for g in range(NG):
                    gi = ex * NG + g
                    tok_gath = None
                    for j in range(2):
                        ch = ex * CPE + g * 2 + j
                        dep = ([tok_disp] + tok_base_list + tok_st_all) if gi == 0 and j == 0 else []
                        if tok_fm_prev is not None and j == 0:
                            dep = dep + [tok_fm_prev]
                        tok_gath = b.dma_op("gpsimd", "ga", lambda e, ch=ch, j=j: e.indirect_dma_start(
                            xe_tm[:, j, :], None, xn_scr[:, :],
                            IndirectOffsetOnAxis(disp_i32[:, ch:ch + 1], 0)), after=dep)
                    deps = [tok_gath, tok_h_done.get(gi - 1)]
                    tok_fm = None
                    for j in range(2):
                        for c4 in range(2):
                            for cc in range(4):
                                c = c4 * 4 + cc
                                tok_tr = b.op("tensor", lambda e, j=j, c=c, cc=cc: e.transpose(
                                    ps_smb[:, cc * 128:(cc + 1) * 128],
                                    xe_tm[:, j, c * 128:(c + 1) * 128], ident_bf[:, :]), after=deps)
                                deps = []
                            tok_fm = b.op("vector", lambda e, j=j, c4=c4: e.tensor_copy(
                                xe_fm[:, c4 * 4:(c4 + 1) * 4, j * 128:(j + 1) * 128],
                                ps_smb[:, 0:512].rearrange("p (c n) -> p c n", c=4)), after=[tok_tr])
                    tok_fm_prev = tok_fm
                    tok_h = None
                    for fs in range(8):
                        wbuf = fs % 2
                        if True:
                            dep_w = [tok_hm_last.get((gi - 1, fs + 6)) if fs < 2 else tok_hm_last.get((gi, fs - 2))
                                     ] + (tok_base_list if gi == 0 and fs < 2 else [])
                            tok_w1[(gi, fs)] = b.dma("gpsimd", "w1a" if wbuf == 0 else "w1b",
                                                     w1s[:, wbuf, :, :],
                                                     w1_in.ap()[ex].rearrange("(c p) f -> p c f", p=128)[
                                                         :, :, fs * 512:(fs + 1) * 512],
                                                     after=dep_w)
                        for fc in range(4):
                            kk = fs * 4 + fc
                            for c in range(8):
                                for (n0, n1) in ((0, 256),):
                                    tok_hm = b.op("tensor", lambda e, wbuf=wbuf, fc=fc, c=c, n0=n0, n1=n1: e.matmul(
                                        ps_big[:, n0:n1],
                                        w1s[:, wbuf, c, fc * 128:(fc + 1) * 128], xe_fm[:, c, n0:n1],
                                        start=(c == 0), stop=(c == 7)),
                                        after=[tok_fm, tok_w1[(gi, fs)]] if (fc == 0 and c == 0 and n0 == 0) else [])
                            tok_hm_last[(gi, fs)] = tok_hm
                            tok_h = b.op("vector", lambda e, kk=kk: e.tensor_scalar(
                                h_buf[:, kk, :], ps_big[:, 0:256], 0.0, None, OP.max), after=[tok_hm])
                    tok_h_done[gi] = tok_h
                    for tc in range(2):
                        ch = ex * CPE + g * 2 + tc
                        ybuf = tc % 2
                        for kk in range(32):
                            for nh in range(2):
                                tok_ym = b.op("tensor", lambda e, kk=kk, tc=tc, nh=nh: e.matmul(
                                    ps_big[:, 1024 + nh * 512:1024 + (nh + 1) * 512],
                                    h_buf[:, kk, tc * 128:(tc + 1) * 128],
                                    w2s[:, kk, nh * 512:(nh + 1) * 512],
                                    start=(kk == 0), stop=(kk == 31)),
                                    after=[tok_h_done[gi], tok_w2[ex]] if (kk == 0 and nh == 0) else [])
                        tok_ye = b.op("vector", lambda e, ybuf=ybuf, ch=ch: e.tensor_scalar(
                            y_stage[:, ybuf, :], ps_big[:, 1024:2048], scale_sb[:, ch:ch + 1], None, OP.mult),
                            after=[tok_ym, tok_scale, tok_ysc])
                        tok_ysc = b.dma_op("gpsimd", "ys", lambda e, ybuf=ybuf, ch=ch: e.indirect_dma_start(
                            out_hbm[:, :], IndirectOffsetOnAxis(disp_i32[:, ch:ch + 1], 0),
                            y_stage[:, ybuf, :], None, compute_op=OP.add),
                            after=[tok_ye] + tok_base_list)
                tok_y_done[ex] = tok_ysc

            for ch in ("ch:ys", "ch:oa", "ch:ob", "ch:oc"):
                if ch in b.ch_cnt:
                    for eng in ("sync", "gpsimd", "vector"):
                        b._wait(eng, (ch, b.ch_cnt[ch]))

        b.finalize()
    return nc


def _host_inputs(N, W_qkv, W_out, proj, Wr):
    import ml_dtypes
    NS = N // 128
    ident = np.eye(128)
    d = {
        "ident_bf": ident.astype(ml_dtypes.bfloat16),
        "ident_f": ident.astype(np.float32),
        "iota8": np.broadcast_to(np.arange(8, dtype=np.float32)[None, :], (128, 8)).copy(),
        "iotacol": np.broadcast_to(np.arange(NS, dtype=np.float32)[None, :], (128, NS)).copy(),
        "onesf": np.ones((128, 128), np.float32),
        "ones1b": np.ones((1, 128), ml_dtypes.bfloat16),
    }
    hsel = np.zeros((16, 2048), np.float32)
    for h in range(16):
        hsel[h, h * 128:(h + 1) * 128] = 1.0
    d["hsel"] = hsel.astype(ml_dtypes.bfloat16)
    d["w_qkv"] = np.ascontiguousarray(W_qkv.reshape(8, 128, 3 * D))
    d["w_out"] = np.ascontiguousarray(W_out.reshape(8, 128, D))
    d["wr"] = np.ascontiguousarray(Wr.reshape(8, 128, E))
    pt = np.ascontiguousarray(proj.transpose(0, 2, 1))
    pmpair = np.zeros((128, 8, 256), np.float32)
    for c in range(8):
        pmpair[0:64, c, 0:128] = pt[2 * c]
        pmpair[64:128, c, 128:256] = pt[2 * c + 1]
    d["pmpair"] = pmpair
    pmfm = np.zeros((128, H, 128), np.float32)
    for h in range(H):
        half = 64 * (h % 2)
        pmfm[half:half + 64, h, :] = pt[h]
    d["pmfm"] = pmfm
    return d


def kernel(L0_attn_x, L0_attn_W_qkv, L0_attn_W_out, L0_attn_proj,
           L0_ffn_Wr, L0_ffn_W1, L0_ffn_W2):
    x = np.asarray(L0_attn_x, np.float32)
    N = x.shape[1]
    C_CORE = 1280
    nc = build_program(N, C_CORE, skip_attn=SKIP_ATTN)
    shared = _host_inputs(N, np.asarray(L0_attn_W_qkv, np.float32),
                          np.asarray(L0_attn_W_out, np.float32),
                          np.asarray(L0_attn_proj, np.float32),
                          np.asarray(L0_ffn_Wr, np.float32))
    shared["w1"] = np.ascontiguousarray(np.asarray(L0_ffn_W1, np.float32))
    shared["w2"] = np.ascontiguousarray(np.asarray(L0_ffn_W2, np.float32))
    in_maps = []
    for c in range(8):
        m = dict(shared)
        m["x"] = np.ascontiguousarray(x[c])
        m["rmask"] = (np.arange(8) < c).astype(np.float32)[:, None]
        in_maps.append(m)
    res = run_bass_kernel_spmd(nc, in_maps, core_ids=list(range(8)))
    return np.stack([res.results[c]["out"] for c in range(8)], axis=0)


# version 04
# speedup vs baseline: 1.0529x; 1.0529x over previous
import math
import numpy as np
from contextlib import ExitStack

import concourse.bass as bass
from concourse.bass import IndirectOffsetOnAxis
import concourse.mybir as mybir
from concourse.bass_utils import run_bass_kernel_spmd

F32 = mybir.dt.float32
BF16 = mybir.dt.bfloat16
I32 = mybir.dt.int32
AX = mybir.AxisListType.X
OP = mybir.AluOpType
AF = mybir.ActivationFunctionType

D = 1024
H = 16
E = 8
FF = 4096
LN16 = float(np.float32(math.log(16.0)))
EPS_ATTN = 1e-6
EPS_LN = 1e-5
SKIP_ATTN = False
PHASES = "all"
RACECHK = False
SERIAL = False

ENGINES = ("sync", "gpsimd", "tensor", "vector", "scalar")


class Builder:

    def __init__(self, nc, ctx):
        self.nc = nc
        self.ctx = ctx
        self.items = {k: [] for k in ENGINES}
        self.sem = {k: ctx.enter_context(nc.semaphore(f"sem_{k}")) for k in ENGINES}
        self.cnt = {k: 0 for k in ENGINES}
        self.waited = {k: {} for k in ENGINES}
        self.waitvals = {}
        self.csem = ctx.enter_context(nc.semaphore("sem_cc"))
        self.ccnt = 0
        self.ch_sem = {}
        self.ch_cnt = {}
        self.audit = []
        self.events = {k: [] for k in ENGINES}

    def _wait(self, eng, tok):
        peng, val = tok
        if peng == eng:
            return
        if peng == "ch:gw":
            val = self.ch_cnt[peng]
        if val > self.waited[eng].get(peng, 0):
            self.waited[eng][peng] = val
            self.audit.append((eng, peng, val))
            self.events[eng].append(("wait", peng, val))
            self.waitvals.setdefault(peng, set()).add(val)
            self.items[eng].append(("wait", peng, val))

    def _serial_waits(self, eng):
        for pe in ENGINES:
            if pe != eng and self.cnt[pe] > 0:
                self._wait(eng, (pe, self.cnt[pe]))
        for ch, cv in self.ch_cnt.items():
            if cv > 0:
                self._wait(eng, (ch, cv))

    def op(self, eng, fn, after=(), inc=1, auto=True):
        if SERIAL:
            self._serial_waits(eng)
        if eng == "tensor" and auto:
            self._wait(eng, ("vector", self.cnt["vector"]))
            self._wait(eng, ("scalar", self.cnt["scalar"]))
        for tok in after:
            if tok is not None:
                self._wait(eng, tok)
        self.cnt[eng] += 1
        self.events[eng].append(("inc", eng, 1))
        self.items[eng].append(("op", fn, self.cnt[eng]))
        return (eng, self.cnt[eng])

    def _dma_common(self, eng, ch, fn, after):
        ch = "ch:" + ch
        if ch not in self.ch_sem:
            self.ch_sem[ch] = self.ctx.enter_context(self.nc.semaphore(ch))
            self.ch_cnt[ch] = 0
        if SERIAL:
            self._serial_waits(eng)
        for tok in after:
            if tok is not None:
                self._wait(eng, tok)
        self.ch_cnt[ch] += 16
        self.events[eng].append(("inc", ch, 16))
        self.items[eng].append(("dma", fn, ch))
        return (ch, self.ch_cnt[ch])

    def dma(self, eng, ch, out_ap, in_ap, after=()):
        return self._dma_common(eng, ch,
                                lambda e, o=out_ap, i=in_ap: e.dma_start(out=o, in_=i), after)

    def dma_op(self, eng, ch, fn, after=()):
        return self._dma_common(eng, ch, fn, after)

    def collective(self, eng, ins_ap, outs_ap, after=()):
        for tok in after:
            if tok is not None:
                self._wait(eng, tok)
        self.ccnt += 1
        self.events[eng].append(("inc", "cc", 1))

        def run(e, i=ins_ap, o=outs_ap, cs=self.csem):
            e.collective_compute("AllGather", OP.bypass, replica_groups=[list(range(8))],
                                 ins=[i], outs=[o]).then_inc(cs, 1)
        self.items[eng].append(("raw", run))
        return ("cc", self.ccnt)

    def tok(self, eng):
        return (eng, self.cnt[eng])

    def finalize(self):
        ranks = {}
        for key, vals in self.waitvals.items():
            if key in ENGINES:
                if RACECHK:
                    vals = set(range(1, self.cnt[key] + 1))
                sv = sorted(vals)
                ranks[key] = {v: i + 1 for i, v in enumerate(sv)}
        if RACECHK:
            for key in ENGINES:
                ranks.setdefault(key, {v: v for v in range(1, self.cnt[key] + 1)})
        with self.nc.Block() as block:
            for name in ENGINES:
                items = self.items[name]
                eng_thresholds = ranks.get(name, {})
                if RACECHK:
                    eng_thresholds = {v: v for v in range(1, self.cnt[name] + 1)}
                sem = self.sem[name]

                def body(e, items=items, eng_thresholds=eng_thresholds, sem=sem, name=name):
                    for it in items:
                        kind = it[0]
                        if kind == "wait":
                            _, peng, val = it
                            if peng == "cc":
                                e.wait_ge(self.csem, val)
                            elif peng in ENGINES:
                                e.wait_ge(self.sem[peng], ranks[peng][val])
                            else:
                                e.wait_ge(self.ch_sem[peng], val)
                        elif kind == "op":
                            _, fn, cntv = it
                            if RACECHK and cntv > 1:
                                e.wait_ge(sem, cntv - 1)
                            inst = fn(e)
                            if cntv in eng_thresholds:
                                inst.then_inc(sem, 1)
                            if name == "vector":
                                e.engine_nop(); e.engine_nop(); e.engine_nop()
                            elif name in ("scalar", "gpsimd"):
                                e.nop()
                        elif kind == "dma":
                            _, fn, ch = it
                            fn(e).then_inc(self.ch_sem[ch], 16)
                        else:
                            it[1](e)
                getattr(block, name)(body)


def build_program(N, C_CORE, skip_attn=False):
    NS = N // 128
    NSLOT = E * C_CORE
    NCHUNK = NSLOT // 128
    CPE = C_CORE // 128
    PDEAD = NSLOT + NS
    PD2 = (PDEAD * 2 + 127) // 128
    CAPF = float(N)
    NG = C_CORE // 256
    assert C_CORE % 256 == 0 and N % 512 == 0

    nc = bass.Bass("TRN2", target_bir_lowering=False, debug=False, detect_race_conditions=RACECHK)
    PHASES_ = PHASES

    x_in = nc.dram_tensor("x", [N, D], F32, kind="ExternalInput")
    w_qkv = nc.dram_tensor("w_qkv", [8, 128, 3 * D], F32, kind="ExternalInput")
    w_out_in = nc.dram_tensor("w_out", [8, 128, D], F32, kind="ExternalInput")
    pmpair_in = nc.dram_tensor("pmpair", [128, 8, 256], F32, kind="ExternalInput")
    pmfm_in = nc.dram_tensor("pmfm", [128, H, 128], F32, kind="ExternalInput")
    wr_in = nc.dram_tensor("wr", [8, 128, E], F32, kind="ExternalInput")
    w1_in = nc.dram_tensor("w1", [E, D, FF], F32, kind="ExternalInput")
    w2_in = nc.dram_tensor("w2", [E, FF, D], F32, kind="ExternalInput")
    ident_bf_in = nc.dram_tensor("ident_bf", [128, 128], BF16, kind="ExternalInput")
    ident_f_in = nc.dram_tensor("ident_f", [128, 128], F32, kind="ExternalInput")
    iota8_in = nc.dram_tensor("iota8", [128, 8], F32, kind="ExternalInput")
    iotacol_in = nc.dram_tensor("iotacol", [128, NS], F32, kind="ExternalInput")
    hsel_in = nc.dram_tensor("hsel", [16, 2048], BF16, kind="ExternalInput")
    onesf_in = nc.dram_tensor("onesf", [128, 128], F32, kind="ExternalInput")
    ones1b_in = nc.dram_tensor("ones1b", [1, 128], BF16, kind="ExternalInput")
    rmask_in = nc.dram_tensor("rmask", [8, 1], F32, kind="ExternalInput")

    out_hbm = nc.dram_tensor("out", [N, D], F32, kind="ExternalOutput")
    DBG = PHASES.startswith("kr")
    if DBG:
        dbg_out = nc.dram_tensor("dbg", [6, 128, NS], F32, kind="ExternalOutput")
    DBGA = PHASES == "adbg"
    if DBGA or PHASES == "lndbg":
        dbga_out = nc.dram_tensor("dbga", [6, 128, 4096], F32, kind="ExternalOutput")
        scr_out = nc.dram_tensor("scr", [2, 256, D], BF16, kind="ExternalOutput")
        scr2_out = nc.dram_tensor("scr2", [4, 128, D], F32, kind="ExternalOutput")

    xn_scr = nc.dram_tensor("xn_scr", [N, D], BF16)
    q_scr = nc.dram_tensor("q_scr", [N, D], BF16)
    pairs_dram = nc.dram_tensor("pairs_dram", [128, PD2], F32)
    cc_in = nc.dram_tensor("cc_in", [8, 1], F32)
    cc_out = nc.dram_tensor("cc_out", [64, 1], F32)

    with ExitStack() as top:
        e_ = top.enter_context
        b = Builder(nc, top)

        pmpair = e_(nc.sbuf_tensor("pmpair_sb", [128, 8, 256], BF16))
        pmfm = e_(nc.sbuf_tensor("pmfm_sb", [128, H, 128], BF16))
        pmpair_n = e_(nc.sbuf_tensor("pmpairn_sb", [128, 8, 256], BF16))
        pmfm_n = e_(nc.sbuf_tensor("pmfmn_sb", [128, H, 128], BF16))
        ident_bf = e_(nc.sbuf_tensor("identbf_sb", [128, 128], BF16))
        ident_f = e_(nc.sbuf_tensor("identf_sb", [128, 128], F32))
        iota8 = e_(nc.sbuf_tensor("iota8_sb", [128, 8], F32))
        iotacol = e_(nc.sbuf_tensor("iotacol_sb", [128, NS], F32))
        hsel = e_(nc.sbuf_tensor("hsel_sb", [16, 2048], BF16))
        onesf = e_(nc.sbuf_tensor("onesf_sb", [128, 128], F32))
        ones1b = e_(nc.sbuf_tensor("ones1b_sb", [1, 128], BF16))
        rmask = e_(nc.sbuf_tensor("rmask_sb", [8, 1], F32))
        e_st = e_(nc.sbuf_tensor("e_st", [128, NS], F32))
        p_st = e_(nc.sbuf_tensor("p_st", [128, NS], F32))
        onehotT = e_(nc.sbuf_tensor("onehotT", [8, N], BF16))
        kv_sb = e_(nc.sbuf_tensor("kv_sb", [65, H, 256], F32))
        negdq_tm = e_(nc.sbuf_tensor("negdq_tm", [128, NS, 16], F32))
        ugmax = e_(nc.sbuf_tensor("ugmax", [128, 16], F32))
        gmln = e_(nc.sbuf_tensor("gmln", [16, 1], F32))
        disp_i32 = e_(nc.sbuf_tensor("disp_i32", [128, NCHUNK], I32))
        scale_sb = e_(nc.sbuf_tensor("scale_sb", [128, NCHUNK], F32))

        ps_big = e_(nc.psum_tensor("ps_big", [128, 2048], F32))
        ps_lg = e_(nc.psum_tensor("ps_lg", [128, 8], F32))
        ps_kvat = e_(nc.psum_tensor("ps_kvat", [65, 512], F32))
        ps_kv = ps_kvat[:, 0:256]
        ps_at = ps_kvat[:, 0:512]
        ps_sm = e_(nc.psum_tensor("ps_sm", [128, 512], F32))
        ps_smb = e_(nc.psum_tensor("ps_smb", [128, 1024], BF16))

        for dst, src in ((ident_bf[:, :], ident_bf_in[:, :]), (ident_f[:, :], ident_f_in[:, :]),
                         (iota8[:, :], iota8_in[:, :]), (iotacol[:, :], iotacol_in[:, :]),
                         (hsel[:, :], hsel_in[:, :]), (onesf[:, :], onesf_in[:, :]),
                         (ones1b[:, :], ones1b_in[:, :]), (rmask[:, :], rmask_in[:, :])):
            tok_consts = b.dma("sync", "c0", dst, src)
        tok_ug0 = b.op("vector", lambda e: e.memset(ugmax[:, :], -1e30))
        tok_kv0 = b.op("vector", lambda e: e.memset(kv_sb.ap(), 0.0))

        with ExitStack() as phk:
            k_ = phk.enter_context
            wqkv_sb = k_(nc.sbuf_tensor("wqkv_sb", [128, 8, 3 * D], BF16))
            wr_sb = k_(nc.sbuf_tensor("wr_sb", [128, 8, E], BF16))
            x_sub = k_(nc.sbuf_tensor("x_sub", [128, 2, D], F32))
            sq_bf = k_(nc.sbuf_tensor("sq_bf", [128, D], BF16))
            lncol = k_(nc.sbuf_tensor("lncol", [128, 8], F32))
            xn_tm = k_(nc.sbuf_tensor("xn_tm", [128, 2, D], BF16))
            xn_fm = k_(nc.sbuf_tensor("xn_fm", [128, 8, 128], BF16))
            q_tm = k_(nc.sbuf_tensor("q_tm", [128, 2, D], BF16))
            k_tm = k_(nc.sbuf_tensor("k_tm", [128, D], BF16))
            k_fm = k_(nc.sbuf_tensor("k_fm", [128, 8, 512], BF16))
            kfp = k_(nc.sbuf_tensor("kfp", [128, 4, H, 128], BF16))
            kfn = k_(nc.sbuf_tensor("kfn", [128, 4, H, 128], BF16))
            vbuf = k_(nc.sbuf_tensor("vbuf", [128, 4, H, 65], BF16))
            negdk_tm = k_(nc.sbuf_tensor("negdk_tm", [128, 16], F32))
            negdk_rows = k_(nc.sbuf_tensor("negdk_rows", [16, 512], BF16))
            lgf = k_(nc.sbuf_tensor("lgf", [128, 8], F32))
            lgtmp = k_(nc.sbuf_tensor("lgtmp", [128, 8], F32))
            lncol2 = k_(nc.sbuf_tensor("lncol2", [128, 4], F32))
            ohtile = k_(nc.sbuf_tensor("ohtile", [128, 8], BF16))
            lnr = k_(nc.sbuf_tensor("lnr", [128, 16], F32))
            dcol = k_(nc.sbuf_tensor("dcol", [128, 16], F32))

            tok_wq = b.dma("gpsimd", "gw", wqkv_sb.ap(), w_qkv.ap().rearrange("c p j -> p c j"))
            tok_wr = b.dma("gpsimd", "gw", wr_sb.ap(), wr_in.ap().rearrange("c p j -> p c j"))
            tok_pp0 = b.dma("gpsimd", "gw", pmpair.ap(), pmpair_in.ap())
            tok_pf0 = b.dma("gpsimd", "gw", pmfm.ap(), pmfm_in.ap())
            tok_pp = b.op("vector", lambda e: e.tensor_scalar(pmpair.ap(), pmpair.ap(), 0.125, None, OP.mult),
                          after=[tok_pf0])
            tok_pf = b.op("vector", lambda e: e.tensor_scalar(pmfm.ap(), pmfm.ap(), 0.125, None, OP.mult))
            tok_pp = b.op("vector", lambda e: e.tensor_scalar(pmpair_n.ap(), pmpair.ap(), -1.0, None, OP.mult))
            tok_pf = b.op("vector", lambda e: e.tensor_scalar(pmfm_n.ap(), pmfm.ap(), -1.0, None, OP.mult))
            tok_vb0 = b.op("vector", lambda e: e.memset(vbuf.ap(), 1.0), after=[tok_consts])

            tok_x = {0: b.dma("sync", "xa", x_sub[:, 0, :], x_in[0:128, :])}
            tok_xn = {}
            tok_xnst = {}
            tok_qst = {}
            tok_qev_d = {}
            tok_kv_done = tok_kv0

            for s in range(NS):
                si = s % 4
                buf = s % 2
                b.op("vector", lambda e, bu=buf: e.reduce_sum(lncol[:, 0:1], x_sub[:, bu, :], AX),
                     after=[tok_x[s]])
                b.op("vector", lambda e, bu=buf: e.tensor_mul(sq_bf[:, :], x_sub[:, bu, :], x_sub[:, bu, :]))
                b.op("vector", lambda e: e.reduce_sum(lncol[:, 1:2], sq_bf[:, :], AX))
                b.op("vector", lambda e: e.tensor_scalar(lncol[:, 2:3], lncol[:, 0:1], 1.0 / D, None, OP.mult))
                b.op("vector", lambda e: e.tensor_mul(lncol[:, 3:4], lncol[:, 2:3], lncol[:, 2:3]))
                b.op("vector", lambda e: e.tensor_scalar(lncol[:, 4:5], lncol[:, 1:2], 1.0 / D, EPS_LN, OP.mult, OP.add))
                tok_var = b.op("vector", lambda e: e.tensor_sub(lncol[:, 4:5], lncol[:, 4:5], lncol[:, 3:4]))
                b.op("scalar", lambda e: e.activation(lncol[:, 5:6], lncol[:, 4:5], AF.Ln),
                     after=[tok_var, tok_consts])
                tok_rstd = b.op("scalar", lambda e: e.activation(lncol[:, 6:7], lncol[:, 5:6], AF.Exp, scale=-0.5))
                tok_xn[s] = b.op("vector", lambda e, bu=buf: e.tensor_scalar(
                    xn_tm[:, bu, :], x_sub[:, bu, :], lncol[:, 2:3], lncol[:, 6:7], OP.subtract, OP.mult),
                    after=[tok_rstd, tok_xnst.get(s - 2)])
                tok_xnst[s] = b.dma("sync", "sna" if buf == 0 else "snb",
                                    xn_scr[s * 128:(s + 1) * 128, :], xn_tm[:, buf, :],
                                    after=[tok_xn[s]])
                if (DBGA or PHASES == "lndbg") and s == 0:
                    b.dma("gpsimd", "dbga", scr2_out.ap()[0], x_sub[:, 0, :], after=[tok_xn[0]])
                    b.dma("gpsimd", "dbga", scr2_out.ap()[1][:, 0:1024], sq_bf[:, :], after=[tok_xn[0]])
                    b.dma("gpsimd", "dbga", scr2_out.ap()[2][:, 0:8], lncol[:, :], after=[tok_xn[0]])
                    b.dma("gpsimd", "dbga", scr2_out.ap()[3][:, 0:1024], xn_tm[:, 0, :], after=[tok_xn[0]])
                    if PHASES == "lndbg":
                        b._wait("sync", ("ch:dbga", b.ch_cnt["ch:dbga"]))
                        b._wait("vector", ("ch:dbga", b.ch_cnt["ch:dbga"]))
                        b.finalize()
                        return nc
                if s + 1 < NS:
                    tok_x[s + 1] = b.dma("sync", "xa" if (s + 1) % 2 == 0 else "xb",
                                         x_sub[:, 1 - buf, :],
                                         x_in[(s + 1) * 128:(s + 2) * 128, :],
                                         after=[tok_xn.get(s - 1)])

                deps = [tok_xn[s], tok_consts]
                for c4 in range(2):
                    for cc in range(4):
                        c = c4 * 4 + cc
                        tok_tr = b.op("tensor", lambda e, bu=buf, c=c, cc=cc: e.transpose(
                            ps_smb[:, cc * 128:(cc + 1) * 128],
                            xn_tm[:, bu, c * 128:(c + 1) * 128], ident_bf[:, :]), after=deps)
                        deps = []
                    tok_fm = b.op("vector", lambda e, c4=c4: e.tensor_copy(
                        xn_fm[:, c4 * 4:(c4 + 1) * 4, :],
                        ps_smb[:, 0:512].rearrange("p (c n) -> p c n", c=4)), after=[tok_tr])
                tok_xnfm = tok_fm

                for c in range(8):
                    for nh in range(2):
                        tok_qmm = b.op("tensor", lambda e, c=c, nh=nh: e.matmul(
                            ps_big[:, nh * 512:(nh + 1) * 512], xn_fm[:, c, :],
                            wqkv_sb[:, c, nh * 512:(nh + 1) * 512],
                            start=(c == 0), stop=(c == 7)),
                            after=[tok_xnfm, tok_wq] if (c == 0 and nh == 0) else [])
                tok_qev = b.op("vector", lambda e, bu=buf: e.tensor_copy(q_tm[:, bu, :], ps_big[:, 0:1024]),
                               after=[tok_qmm, tok_qst.get(s - 2)])
                tok_qst[s] = b.dma("sync", "qsa" if buf == 0 else "qsb",
                                   q_scr[s * 128:(s + 1) * 128, :], q_tm[:, buf, :], after=[tok_qev])
                if not skip_attn:
                    b.op("vector", lambda e, bu=buf: e.tensor_mul(sq_bf[:, :], q_tm[:, bu, :], q_tm[:, bu, :]))
                    b.op("vector", lambda e: e.tensor_reduce(
                        dcol[:, :], sq_bf.ap().rearrange("p (h n) -> p h n", h=16), AX, OP.add))
                    b.op("vector", lambda e, s=s: e.tensor_scalar(
                        negdq_tm[:, s, :], dcol[:, :], -0.5, -LN16, OP.mult, OP.add))

                    for c in range(8):
                        for nh in range(2):
                            tok_kmm = b.op("tensor", lambda e, c=c, nh=nh: e.matmul(
                                ps_big[:, 1024 + nh * 512:1024 + (nh + 1) * 512], xn_fm[:, c, :],
                                wqkv_sb[:, c, D + nh * 512:D + (nh + 1) * 512],
                                start=(c == 0), stop=(c == 7)), after=[])
                    tok_kev = b.op("vector", lambda e: e.tensor_copy(k_tm[:, :], ps_big[:, 1024:2048]),
                                   after=[tok_kmm])
                    b.op("vector", lambda e: e.tensor_mul(sq_bf[:, :], k_tm[:, :], k_tm[:, :]))
                    b.op("vector", lambda e: e.tensor_reduce(
                        dcol[:, :], sq_bf.ap().rearrange("p (h n) -> p h n", h=16), AX, OP.add))
                    tok_ndk = b.op("vector", lambda e: e.tensor_scalar(
                        negdk_tm[:, :], dcol[:, :], -0.5, -LN16, OP.mult, OP.add))
                    tok_ndt = b.op("tensor", lambda e: e.transpose(
                        ps_sm[0:16, 0:128], negdk_tm[:, :], ident_f[:, :]), after=[tok_ndk])
                    tok_ndr = b.op("vector", lambda e, si=si: e.tensor_copy(
                        negdk_rows[:, si * 128:(si + 1) * 128], ps_sm[0:16, 0:128]), after=[tok_ndt])
                    for c4 in range(2):
                        for cc in range(4):
                            c = c4 * 4 + cc
                            tok_tr2 = b.op("tensor", lambda e, c=c, cc=cc: e.transpose(
                                ps_smb[:, cc * 128:(cc + 1) * 128],
                                k_tm[:, c * 128:(c + 1) * 128], ident_bf[:, :]), after=[])
                        tok_kfm = b.op("vector", lambda e, c4=c4, si=si: e.tensor_copy(
                            k_fm[:, c4 * 4:(c4 + 1) * 4, si * 128:(si + 1) * 128],
                            ps_smb[:, 0:512].rearrange("p (c n) -> p c n", c=4)), after=[tok_tr2])
                    tok_kfm_done = tok_kfm

                for c in range(8):
                    for nh in range(2):
                        tok_vmm = b.op("tensor", lambda e, c=c, nh=nh: e.matmul(
                            ps_big[:, nh * 512:(nh + 1) * 512], xn_fm[:, c, :],
                            wqkv_sb[:, c, 2 * D + nh * 512:2 * D + (nh + 1) * 512],
                            start=(c == 0), stop=(c == 7)), after=[tok_wr] if (c == 0 and nh == 0) else [])
                    tok_lmm = b.op("tensor", lambda e, c=c: e.matmul(
                        ps_lg[:, :], xn_fm[:, c, :], wr_sb[:, c, :],
                        start=(c == 0), stop=(c == 7)))
                if not skip_attn:
                    tok_vev = b.op("vector", lambda e, si=si: e.tensor_copy(
                        vbuf[:, si, :, 0:64],
                        ps_big[:, 0:1024].rearrange("p (h n) -> p h n", h=16)),
                        after=[tok_vmm, tok_vb0])

                b.op("vector", lambda e: e.tensor_copy(lgf[:, :], ps_lg[:, :]), after=[tok_lmm])
                b.op("vector", lambda e: e.reduce_max(lncol2[:, 3:4], lgf[:, :], AX))
                tok_nm = b.op("vector", lambda e: e.tensor_scalar(lncol2[:, 0:1], lncol2[:, 3:4], -1.0, None, OP.mult))
                tok_ex = b.op("scalar", lambda e: e.activation(lgtmp[:, :], lgf[:, :], AF.Exp, bias=lncol2[:, 0:1]),
                              after=[tok_nm])
                tok_sm = b.op("vector", lambda e: e.reduce_sum(lncol2[:, 1:2], lgtmp[:, :], AX), after=[tok_ex])
                b.op("vector", lambda e, s=s: e.reciprocal(p_st[:, s:s + 1], lncol2[:, 1:2]))
                b.op("vector", lambda e: e.tensor_single_scalar(lgtmp[:, :], lgf[:, :], lncol2[:, 3:4], OP.is_lt))
                b.op("vector", lambda e: e.scalar_tensor_tensor(lgtmp[:, :], lgtmp[:, :], 1000.0, iota8[:, :], OP.mult, OP.add))
                tok_e = b.op("vector", lambda e, s=s: e.tensor_reduce(e_st[:, s:s + 1], lgtmp[:, :], AX, OP.min))
                tok_oh = b.op("vector", lambda e, s=s: e.tensor_single_scalar(
                    ohtile[:, :], iota8[:, :], e_st[:, s:s + 1], OP.is_equal))
                tok_oht = b.op("tensor", lambda e: e.transpose(ps_smb[0:8, 0:128], ohtile[:, :], ident_bf[:, :]),
                               after=[tok_oh])
                b.op("vector", lambda e, s=s: e.tensor_copy(
                    onehotT[:, s * 128:(s + 1) * 128], ps_smb[0:8, 0:128]), after=[tok_oht])

                if not skip_attn:
                    tok_kf2 = None
                    for half in range(2):
                        hs = hsel[:, half * 1024:(half + 1) * 1024]
                        for pn in range(2):
                            for nh in range(2):
                                tok_h2 = b.op("tensor", lambda e, si=si, nh=nh, pn=pn, hs=hs: e.matmul(
                                    ps_big[:, pn * 1024 + nh * 512:pn * 1024 + (nh + 1) * 512],
                                    negdk_rows[:, si * 128:(si + 1) * 128], hs[:, nh * 512:(nh + 1) * 512],
                                    start=True, stop=False),
                                    after=[tok_vev, tok_kfm_done, tok_pp, tok_ndr, tok_kf2]
                                    if (nh == 0 and pn == 0) else [])
                        for pn, pmx in ((0, pmpair), (1, pmpair_n)):
                            for cc in range(4):
                                c = half * 4 + cc
                                tok_h2 = b.op("tensor", lambda e, c=c, cc=cc, si=si, pn=pn, pmx=pmx: e.matmul(
                                    ps_big[:, pn * 1024 + cc * 256:pn * 1024 + (cc + 1) * 256],
                                    k_fm[:, c, si * 128:(si + 1) * 128], pmx[:, c, :],
                                    start=False, stop=(cc % 2 == 1)))
                        tok_kf1 = b.op("scalar", lambda e, si=si, half=half: e.activation(
                            kfp[:, si, half * 8:(half + 1) * 8, :],
                            ps_big[:, 0:1024].rearrange("p (h n) -> p h n", h=8), AF.Exp),
                            after=[tok_h2])
                        tok_kf2 = b.op("scalar", lambda e, si=si, half=half: e.activation(
                            kfn[:, si, half * 8:(half + 1) * 8, :],
                            ps_big[:, 1024:2048].rearrange("p (h n) -> p h n", h=8), AF.Exp))
                    tok_rm = b.op("vector", lambda e, si=si: e.tensor_reduce(
                        lnr[:, :], kfp[:, si, :, :], AX, OP.max), after=[tok_kf1])
                    tok_ln = b.op("scalar", lambda e: e.activation(lnr[:, :], lnr[:, :], AF.Ln), after=[tok_rm])
                    b.op("vector", lambda e: e.tensor_sub(lnr[:, :], lnr[:, :], negdk_tm[:, :]),
                         after=[tok_ln])
                    b.op("vector", lambda e: e.tensor_max(ugmax[:, :], ugmax[:, :], lnr[:, :]),
                         after=[tok_ug0])

                    if si == 3:
                        tok_prev = None
                        for h in range(H):
                            for si2 in range(4):
                                b.op("tensor", lambda e, h=h, si2=si2: e.matmul(
                                    ps_kv[:, 0:128], vbuf[:, si2, h, :], kfp[:, si2, h, :],
                                    start=(si2 == 0), stop=False),
                                    after=[tok_kf1, tok_kf2, tok_kv_done] if (h == 0 and si2 == 0) else [])
                                tok_kvm = b.op("tensor", lambda e, h=h, si2=si2: e.matmul(
                                    ps_kv[:, 128:256], vbuf[:, si2, h, :], kfn[:, si2, h, :],
                                    start=False, stop=(si2 == 3)))
                            tok_prev = b.op("vector", lambda e, h=h: e.tensor_add(
                                kv_sb[:, h, :], kv_sb[:, h, :], ps_kv[:, :]), after=[tok_kvm])
                        tok_kv_done = tok_prev

            tok_phaseK_v = b.tok("vector")
            tok_st_all = [tok_xnst[NS - 1], tok_xnst[NS - 2], tok_qst[NS - 1], tok_qst[NS - 2]]
            tok_xnst_all = tok_xnst[NS - 1]
            tok_qst_all = tok_qst[NS - 1]
            if PHASES == "k":
                b._wait("sync", tok_phaseK_v)
                b._wait("gpsimd", tok_phaseK_v)
                b.finalize()
                return nc

            zrow8 = k_(nc.sbuf_tensor("zrow8", [8, 512], BF16))
            rkT = k_(nc.sbuf_tensor("rkT", [8, 512], F32))
            rkEx = k_(nc.sbuf_tensor("rkEx", [8, 2, 512], F32))
            carry8 = k_(nc.sbuf_tensor("carry8", [8, 1], F32))
            rk_dram = nc.dram_tensor("rk_dram", [8, N], F32)
            tok_z8 = b.op("vector", lambda e: e.memset(zrow8[:, :], 0.0))
            b.op("vector", lambda e: e.memset(carry8[:, :], 0.0))
            NCH = N // 512
            tok_spill = [None] * NCH
            for ch in range(NCH):
                tok_scan = b.op("vector", lambda e, ch=ch: e.tensor_tensor_scan(
                    rkT[:, :], onehotT[:, ch * 512:(ch + 1) * 512],
                    zrow8[:, :], 0.0, OP.add, OP.add),
                    after=[tok_phaseK_v, tok_z8] if ch == 0 else [])
                if ch > 0:
                    tok_scan = b.op("vector", lambda e, ch=ch: e.tensor_scalar_add(
                        rkT[:, :], rkT[:, :], carry8[:, 0:1]))
                if ch + 1 < NCH:
                    b.op("vector", lambda e, ch=ch: e.tensor_copy(carry8[:, :], rkT[:, 511:512]))
                par = ch % 2
                sub_deps = [tok_spill[ch - 2]] if ch >= 2 else []
                tok_sub = b.op("vector", lambda e, ch=ch, par=par: e.tensor_sub(
                    rkEx[:, par, :], rkT[:, :], onehotT[:, ch * 512:(ch + 1) * 512]),
                    after=sub_deps)
                tok_spill[ch] = b.dma("sync", "rs" + "ab"[par],
                                      rk_dram.ap()[:, ch * 512:(ch + 1) * 512],
                                      rkEx[:, par, :], after=[tok_sub])
            cnt_sb = k_(nc.sbuf_tensor("cnt_sb", [8, 1], F32))
            tok_cnt = b.op("vector", lambda e: e.tensor_copy(cnt_sb[:, :], rkT[:, 511:512]), after=[tok_scan])
            if PHASES == "kr1":
                b._wait("sync", tok_cnt)
                b._wait("gpsimd", tok_cnt)
                b.finalize()
                return nc
            tok_ccin = b.dma("gpsimd", "cb", cc_in.ap(), cnt_sb[:, :], after=[tok_cnt])
            tok_cc = b.collective("gpsimd", cc_in.ap(), cc_out.ap(), after=[tok_ccin])
            allc_sb = k_(nc.sbuf_tensor("allc_sb", [8, 8], F32))
            tok_allc = b.dma("gpsimd", "cb", allc_sb[:, :],
                             cc_out.ap().rearrange("(r e) one -> r (e one)", r=8), after=[tok_cc])
            offrow = k_(nc.sbuf_tensor("offrow", [1, 8], F32))
            offb = k_(nc.sbuf_tensor("offb", [128, 8], F32))
            tok_off = b.op("tensor", lambda e: e.matmul(ps_lg[0:1, 0:8], rmask[:, :], allc_sb[:, :],
                                                        start=True, stop=True), after=[tok_allc])
            tok_offr = b.op("vector", lambda e: e.tensor_copy(offrow[:, :], ps_lg[0:1, 0:8]), after=[tok_off])
            tok_offb_mm = b.op("tensor", lambda e: e.matmul(ps_lg[:, :], onesf[0:1, :], offrow[:, :],
                                                            start=True, stop=True), after=[tok_offr])
            tok_offb = b.op("vector", lambda e: e.tensor_copy(offb[:, :], ps_lg[:, :]), after=[tok_offb_mm])
            if PHASES == "kr2":
                b._wait("sync", tok_offb)
                b._wait("gpsimd", tok_offb)
                b.finalize()
                return nc

            lrank = k_(nc.sbuf_tensor("lrank", [128, NS], F32))
            grank = k_(nc.sbuf_tensor("grank", [128, NS], F32))
            rtmp = k_(nc.sbuf_tensor("rtmp", [128, 8], F32))
            rtmp2 = k_(nc.sbuf_tensor("rtmp2", [128, 8], F32))
            ohre = k_(nc.sbuf_tensor("ohre", [128, 8], F32))
            rkLd = k_(nc.sbuf_tensor("rkLd", [8, 2, 512], F32))
            tok_load = [None] * NCH
            tok_load[0] = b.dma("gpsimd", "rla", rkLd[:, 0, :],
                                rk_dram.ap()[:, 0:512], after=[tok_spill[0]])
            if NCH > 1:
                tok_load[1] = b.dma("gpsimd", "rlb", rkLd[:, 1, :],
                                    rk_dram.ap()[:, 512:1024], after=[tok_spill[1]])
            for s in range(NS):
                g = s // 4
                tok_rt = b.op("tensor", lambda e, s=s, g=g: e.transpose(
                    ps_sm[0:128, 0:8], rkLd[:, g % 2, (s % 4) * 128:(s % 4) * 128 + 128],
                    ident_f[0:8, 0:8]),
                    after=[tok_load[g], tok_offb] if s % 4 == 0 else [])
                if s % 4 == 3 and g + 2 < NCH:
                    tok_load[g + 2] = b.dma("gpsimd", "rl" + "ab"[g % 2],
                                            rkLd[:, g % 2, :],
                                            rk_dram.ap()[:, (g + 2) * 512:(g + 3) * 512],
                                            after=[tok_spill[g + 2], tok_rt])
                b.op("vector", lambda e, s=s: e.tensor_single_scalar(
                    ohre[:, :], iota8[:, :], e_st[:, s:s + 1], OP.is_equal), after=[tok_rt])
                b.op("vector", lambda e: e.tensor_mul(rtmp[:, :], ps_sm[0:128, 0:8], ohre[:, :]))
                b.op("vector", lambda e, s=s: e.reduce_sum(lrank[:, s:s + 1], rtmp[:, :], AX))
                b.op("vector", lambda e: e.tensor_mul(rtmp2[:, :], offb[:, :], ohre[:, :]))
                b.op("vector", lambda e, s=s: e.reduce_sum(grank[:, s:s + 1], rtmp2[:, :], AX))
            tok_gr = b.op("vector", lambda e: e.tensor_add(grank[:, :], grank[:, :], lrank[:, :]))
            keepm = k_(nc.sbuf_tensor("keepm", [128, NS], F32))
            b.op("vector", lambda e: e.tensor_single_scalar(keepm[:, :], grank[:, :], CAPF, OP.is_lt),
                 after=[tok_gr])
            tok_scl = b.op("vector", lambda e: e.tensor_mul(keepm[:, :], keepm[:, :], p_st[:, :]))
            if DBG:
                for i_, tns in enumerate((e_st, p_st, lrank, grank, keepm)):
                    b.dma("gpsimd", "dbgch", dbg_out.ap()[i_], tns[:, :], after=[tok_scl])
            if PHASES == "kr3":
                b._wait("sync", tok_scl)
                b._wait("gpsimd", ("ch:dbgch", b.ch_cnt["ch:dbgch"]))
                b.finalize()
                return nc
            ph_a = k_(nc.sbuf_tensor("ph_a", [128, NS], F32))
            ph_q = k_(nc.sbuf_tensor("ph_q", [128, NS], F32))
            ph_i = k_(nc.sbuf_tensor("ph_i", [128, NS], I32))
            ph_r = k_(nc.sbuf_tensor("ph_r", [128, NS], F32))
            b.op("vector", lambda e: e.tensor_scalar(ph_a[:, :], lrank[:, :], 1.0 / 128.0, None, OP.mult))
            b.op("vector", lambda e: e.tensor_copy(ph_i[:, :], ph_a[:, :]))
            b.op("vector", lambda e: e.tensor_copy(ph_q[:, :], ph_i[:, :]))
            b.op("vector", lambda e: e.scalar_tensor_tensor(ph_r[:, :], ph_q[:, :], -128.0, lrank[:, :], OP.mult, OP.add))
            b.op("vector", lambda e: e.tensor_single_scalar(ph_a[:, :], ph_r[:, :], 0.0, OP.is_lt))
            b.op("vector", lambda e: e.tensor_sub(ph_q[:, :], ph_q[:, :], ph_a[:, :]))
            b.op("vector", lambda e: e.scalar_tensor_tensor(ph_r[:, :], ph_a[:, :], 128.0, ph_r[:, :], OP.mult, OP.add))
            b.op("vector", lambda e: e.scalar_tensor_tensor(ph_q[:, :], e_st[:, :], float(CPE), ph_q[:, :], OP.mult, OP.add))
            b.op("vector", lambda e: e.scalar_tensor_tensor(ph_q[:, :], ph_r[:, :], float(NCHUNK), ph_q[:, :], OP.mult, OP.add))
            b.op("vector", lambda e: e.tensor_single_scalar(ph_a[:, :], lrank[:, :], float(C_CORE) - 0.5, OP.is_gt))
            b.op("vector", lambda e: e.tensor_scalar(ph_r[:, :], iotacol[:, :], 1.0, float(NSLOT), OP.mult, OP.add))
            b.op("vector", lambda e: e.tensor_sub(ph_r[:, :], ph_r[:, :], ph_q[:, :]))
            b.op("vector", lambda e: e.tensor_mul(ph_r[:, :], ph_r[:, :], ph_a[:, :]))
            b.op("vector", lambda e: e.tensor_add(ph_q[:, :], ph_q[:, :], ph_r[:, :]))
            b.op("vector", lambda e: e.tensor_scalar_max(ph_q[:, :], ph_q[:, :], 0.0))
            b.op("vector", lambda e: e.tensor_scalar_min(ph_q[:, :], ph_q[:, :], float(PDEAD - 1)))
            if DBG:
                b.dma("gpsimd", "dbgch", dbg_out.ap()[5], ph_q[:, :], after=[(
                    "vector", b.cnt["vector"])])
            tok_phi = b.op("vector", lambda e: e.tensor_copy(ph_i[:, :], ph_q[:, :]))
            prs = k_(nc.sbuf_tensor("prs", [128, NS, 2], F32))
            ids_f = k_(nc.sbuf_tensor("ids_f", [128, NS], F32))
            tok_iota = b.op("gpsimd", lambda e: e.iota(ids_f[:, :], [[128, NS]], channel_multiplier=1,
                                                       allow_small_or_imprecise_dtypes=True),
                            after=[tok_allc])
            tok_p0 = b.op("vector", lambda e: e.tensor_copy(prs[:, :, 0], ids_f[:, :]), after=[tok_iota])
            tok_p1 = b.op("vector", lambda e: e.tensor_copy(prs[:, :, 1], keepm[:, :]), after=[tok_scl])
            zpair = k_(nc.sbuf_tensor("zpair", [128, PD2], F32))
            tok_zp = b.op("vector", lambda e: e.memset(zpair[:, :], 0.0))
            tok_zpd = b.dma("sync", "pz", pairs_dram.ap(), zpair[:, :], after=[tok_zp])
            if PHASES == "kr4":
                b._wait("sync", tok_p0)
                b._wait("gpsimd", tok_zpd)
                b._wait("gpsimd", tok_p1)
                b.finalize()
                return nc
            pairs_flat = pairs_dram.ap().rearrange("p k -> (p k)")
            tok_sc_last = None
            for s in range(NS):
                deps = [tok_phi, tok_p0, tok_p1, tok_zpd] if s == 0 else []
                tok_sc_last = b.dma_op("gpsimd", "ps", lambda e, s=s: e.indirect_dma_start(
                    pairs_flat.rearrange("(r k) -> r k", k=2), IndirectOffsetOnAxis(ph_i[:, s:s + 1], 0),
                    prs[:, s, :], None), after=deps)
            prs2 = k_(nc.sbuf_tensor("prs2", [128, NCHUNK, 2], F32))
            tok_ld2 = b.dma("sync", "pl", prs2.ap(),
                            pairs_flat[0:NSLOT * 2].rearrange("(p c k) -> p c k", p=128, k=2),
                            after=[tok_sc_last])
            tok_disp = b.op("vector", lambda e: e.tensor_copy(disp_i32[:, :], prs2[:, :, 0]), after=[tok_ld2])
            tok_scale = b.op("vector", lambda e: e.tensor_copy(scale_sb[:, :], prs2[:, :, 1]))
            if PHASES == "kr5":
                b._wait("sync", tok_scale)
                b._wait("gpsimd", tok_scale)
                b.finalize()
                return nc

            if not skip_attn:
                tok_gt = b.op("tensor", lambda e: e.transpose(ps_sm[0:16, 0:128], ugmax[:, :], ident_f[:, :]),
                              after=[])
                tok_gmr = b.op("vector", lambda e: e.tensor_reduce(gmln[:, :], ps_sm[0:16, 0:128], AX, OP.max),
                               after=[tok_gt])
            else:
                tok_gmr = None
            tok_routing_g = b.tok("gpsimd")
            tok_routing_v = b.tok("vector")
            if PHASES == "kr":
                b._wait("sync", tok_scale)
                b._wait("gpsimd", tok_scale)
                b.finalize()
                return nc
            if PHASES == "kra":
                with ExitStack() as tmpq:
                    wout_t = tmpq.enter_context(nc.sbuf_tensor("wout_sb", [128, 8, D], BF16))
                    tok_wo_t = b.dma("gpsimd", "gw", wout_t.ap(), w_out_in.ap().rearrange("c p j -> p c j"),
                                     after=[tok_scale] + tok_st_all + ([tok_gmr] if tok_gmr else []))
                    b._wait("sync", tok_wo_t)
                    b._wait("vector", tok_wo_t)
                    b.finalize()
                return nc

        with ExitStack() as phq:
            q_ = phq.enter_context
            wout_sb = q_(nc.sbuf_tensor("wout_sb", [128, 8, D], BF16))
            tok_wo = b.dma("gpsimd", "gw", wout_sb.ap(), w_out_in.ap().rearrange("c p j -> p c j"), after=[tok_scale, tok_gmr] + tok_st_all)
            tok_outst = {}

            qa_stack = ExitStack()
            qa = qa_stack.enter_context
            if not skip_attn:
                kvaug = q_(nc.sbuf_tensor("kvaug_sb", [128, H, 2, 65], BF16))
                q_sub = qa(nc.sbuf_tensor("q_sub", [128, 2, D], BF16))
                q_fm = qa(nc.sbuf_tensor("q_fm", [128, 8, 512], BF16))
                x2_sub = qa(nc.sbuf_tensor("x2_sub", [128, 4, D], F32))
                stab = qa(nc.sbuf_tensor("stab", [128, 16], F32))
                negs16 = qa(nc.sbuf_tensor("negs16", [16, 128], F32))
                negs_p0 = qa(nc.sbuf_tensor("negs_p0", [1, 16, 2, 512], BF16))
                qfa = qa(nc.sbuf_tensor("qfa", [128, 512], BF16))
                qfb = qa(nc.sbuf_tensor("qfb", [128, 512], BF16))
                zr = qa(nc.sbuf_tensor("zr", [128, 512], F32))
                rb_sb = qa(nc.sbuf_tensor("rb_sb", [64, 512], F32))
                apart = qa(nc.sbuf_tensor("apart", [64, 512], BF16))
                attnbuf = qa(nc.sbuf_tensor("attnbuf", [128, 8, 512], BF16))
                out_sb = qa(nc.sbuf_tensor("out_sb", [128, 2, D], F32))

                tok_kvaug = None
                for h in range(H):
                    for j in range(2):
                        tok_tr = b.op("tensor", lambda e, h=h, j=j: e.transpose(
                            ps_sm[0:128, 0:65], kv_sb[:, h, j * 128:(j + 1) * 128], ident_f[0:65, 0:65]),
                            after=([tok_kv_done, tok_scale, tok_gmr] + tok_st_all) if (h == 0 and j == 0) else [])
                        tok_kvaug = b.op("vector", lambda e, h=h, j=j: e.tensor_copy(
                            kvaug[:, h, j, :], ps_sm[0:128, 0:65]), after=[tok_tr])

                tok_qld = {0: b.dma("sync", "qla", q_sub[:, 0, :], q_scr[0:128, :],
                                     after=[tok_scale, tok_gmr] + tok_st_all)}
                tok_xld = {}
                prev_qfm = {}
                tok_nsd = {}
                tok_oadd_last = None
                for s in range(NS):
                    t, si = s // 4, s % 4
                    buf = s % 2
                    if si == 0:
                        for si2 in range(4):
                            tok_xld[t] = b.dma("sync", "x2", x2_sub[:, si2, :],
                                               x_in[(t * 4 + si2) * 128:(t * 4 + si2 + 1) * 128, :],
                                               after=([tok_oadd_last] if t > 0 else [tok_scale, tok_gmr] + tok_st_all) if si2 == 0 else [])
                    deps = [tok_qld[s], tok_kvaug if s == 0 else None]
                    for c4 in range(2):
                        for cc in range(4):
                            c = c4 * 4 + cc
                            tok_tr = b.op("tensor", lambda e, bu=buf, c=c, cc=cc: e.transpose(
                                ps_smb[:, cc * 128:(cc + 1) * 128],
                                q_sub[:, bu, c * 128:(c + 1) * 128], ident_bf[:, :]), after=deps)
                            deps = []
                        tok_ev = b.op("vector", lambda e, c4=c4, si=si: e.tensor_copy(
                            q_fm[:, c4 * 4:(c4 + 1) * 4, si * 128:(si + 1) * 128],
                            ps_smb[:, 0:512].rearrange("p (c n) -> p c n", c=4)), after=[tok_tr])
                    prev_qfm[s] = tok_ev
                    if s + 1 < NS:
                        tok_qld[s + 1] = b.dma("sync", "qla" if (s + 1) % 2 == 0 else "qlb",
                                               q_sub[:, 1 - buf, :],
                                               q_scr[(s + 1) * 128:(s + 2) * 128, :],
                                               after=[prev_qfm.get(s - 1)])
                    for c in range(8):
                        tok_umm = b.op("tensor", lambda e, c=c, si=si: e.matmul(
                            ps_big[:, c * 256:(c + 1) * 256],
                            q_fm[:, c, si * 128:(si + 1) * 128], pmpair[:, c, :],
                            start=True, stop=True), after=[tok_ev, tok_pp] if c == 0 else [])
                    tok_stab = b.op("vector", lambda e: e.tensor_reduce(
                        stab[:, :], ps_big.ap().rearrange("p (h n) -> p h n", h=16), AX, OP.max),
                        after=[tok_umm])
                    tok_ns = b.op("vector", lambda e, s=s: e.tensor_sub(stab[:, :], negdq_tm[:, s, :], stab[:, :]),
                                  after=[tok_stab])
                    tok_nst = b.op("tensor", lambda e: e.transpose(ps_sm[0:16, 0:128], stab[:, :], ident_f[:, :]),
                                   after=[tok_ns])
                    tok_n16 = b.op("vector", lambda e: e.tensor_scalar(
                        negs16[:, :], ps_sm[0:16, 0:128], gmln[:, 0:1], None, OP.subtract),
                        after=[tok_nst, tok_gmr, tok_nsd.get(s - 1)])
                    tok_nsd[s] = b.dma("gpsimd", "nga" if t % 2 == 0 else "ngb",
                                       negs_p0.ap()[0:1, :, t % 2, si * 128:(si + 1) * 128],
                                       negs16[:, :], after=[tok_n16])

                    if si == 3:
                        tok_ab = None
                        tok_ap_prev = None
                        for h in range(H):
                            b.op("tensor", lambda e, h=h, t=t: e.matmul(
                                ps_big[:, 0:512], ones1b[:, :], negs_p0[0:1, h, t % 2, :],
                                start=True, stop=False),
                                after=[prev_qfm[s], tok_nsd[s], tok_pf] if h == 0 else [])
                            b.op("tensor", lambda e, h=h, t=t: e.matmul(
                                ps_big[:, 512:1024], ones1b[:, :], negs_p0[0:1, h, t % 2, :],
                                start=True, stop=False))
                            b.op("tensor", lambda e, h=h: e.matmul(
                                ps_big[:, 0:512], pmfm[:, h, :], q_fm[:, h // 2, :],
                                start=False, stop=True))
                            tok_bc = b.op("tensor", lambda e, h=h: e.matmul(
                                ps_big[:, 512:1024], pmfm_n[:, h, :], q_fm[:, h // 2, :],
                                start=False, stop=True))
                            tok_qf1 = b.op("scalar", lambda e: e.activation(qfa[:, :], ps_big[:, 0:512], AF.Exp),
                                           after=[tok_bc])
                            tok_qf2 = b.op("scalar", lambda e: e.activation(qfb[:, :], ps_big[:, 512:1024], AF.Exp))
                            tok_am1 = b.op("tensor", lambda e, h=h: e.matmul(
                                ps_at[:, :], kvaug[:, h, 0, :], qfa[:, :], start=True, stop=False),
                                after=[tok_qf1, tok_qf2])
                            tok_am2 = b.op("tensor", lambda e, h=h: e.matmul(
                                ps_at[:, :], kvaug[:, h, 1, :], qfb[:, :], start=False, stop=True))
                            tok_z = b.op("vector", lambda e: e.tensor_scalar(
                                zr[64:65, :], ps_at[64:65, :], 1.0, EPS_ATTN, OP.mult, OP.add),
                                after=[tok_am2])
                            tok_r = b.op("vector", lambda e: e.reciprocal(zr[64:65, :], zr[64:65, :]))
                            tok_rb = b.op("tensor", lambda e: e.matmul(
                                ps_sm[0:64, 0:512], onesf[64:65, 0:64], zr[64:65, :], start=True, stop=True),
                                after=[tok_r])
                            tok_rbe = b.op("vector", lambda e: e.tensor_copy(rb_sb[:, :], ps_sm[0:64, 0:512]),
                                           after=[tok_rb])
                            tok_ap = b.op("vector", lambda e: e.scalar_tensor_tensor(
                                apart[:, :], ps_at[0:64, :], 1.0, rb_sb[:, :], OP.mult, OP.mult),
                                after=[tok_rbe])
                            tok_ap_prev = tok_ap
                            if h % 2 == 0:
                                tok_ab = b.op("vector", lambda e, h=h: e.tensor_copy(
                                    attnbuf[0:64, h // 2, :], apart[:, :]), after=[tok_ap])
                            else:
                                tok_sh = b.op("tensor", lambda e: e.matmul(
                                    ps_big[64:128, 1024:1536], ident_bf[0:64, 0:64], apart[:, :],
                                    start=True, stop=True), after=[tok_ap])
                                tok_ab = b.op("vector", lambda e, h=h: e.tensor_copy(
                                    attnbuf[64:128, h // 2, :], ps_big[64:128, 1024:1536]), after=[tok_sh])
                        for si2 in range(4):
                            s2 = t * 4 + si2
                            bu2 = s2 % 2
                            for c in range(8):
                                for nh in range(2):
                                    tok_wm = b.op("tensor", lambda e, c=c, si2=si2, nh=nh: e.matmul(
                                        ps_big[:, nh * 512:(nh + 1) * 512],
                                        attnbuf[:, c, si2 * 128:(si2 + 1) * 128],
                                        wout_sb[:, c, nh * 512:(nh + 1) * 512],
                                        start=(c == 0), stop=(c == 7)),
                                        after=[tok_ab, tok_wo] if (c == 0 and nh == 0) else [])
                            tok_oadd = b.op("vector", lambda e, bu2=bu2, si2=si2: e.tensor_add(
                                out_sb[:, bu2, :], x2_sub[:, si2, :], ps_big[:, 0:1024]),
                                after=[tok_wm, tok_xld[t], tok_outst.get(s2 - 2)])
                            tok_outst[s2] = b.dma("sync", "oa" if bu2 == 0 else "ob",
                                                  out_hbm[s2 * 128:(s2 + 1) * 128, :],
                                                  out_sb[:, bu2, :], after=[tok_oadd])
                            tok_oadd_last = tok_oadd
                tok_base_done = tok_outst[NS - 1]
                tok_base_list = [tok_outst[NS - 1], tok_outst[NS - 2]]
            else:
                x2_sub = qa(nc.sbuf_tensor("x2_sub", [128, 2, D], F32))
                tok_xld = {0: b.dma("sync", "x2a", x2_sub[:, 0, :], x_in[0:128, :], after=[tok_scale] + tok_st_all + ([tok_gmr] if tok_gmr else []))}
                tok_cp = {}
                for s in range(NS):
                    buf = s % 2
                    if s + 1 < NS:
                        tok_xld[s + 1] = b.dma("sync", "x2a" if (s + 1) % 2 == 0 else "x2b",
                                               x_sub[:, 1 - buf, :] if False else x2_sub[:, 1 - buf, :],
                                               x_in[(s + 1) * 128:(s + 2) * 128, :],
                                               after=[tok_cp.get(s - 1)])
                    if PHASES == "krq0":
                        tok_cp[s] = b.dma("gpsimd", "oc", out_hbm[0:128, :], x_in[0:128, :],
                                          after=[tok_xld[s]])
                    else:
                        tok_cp[s] = b.dma("gpsimd", "oc", out_hbm[s * 128:(s + 1) * 128, :], x2_sub[:, buf, :],
                                          after=[tok_xld[s]])
                tok_base_done = tok_cp[NS - 1]
                tok_base_list = [tok_base_done]

            if DBGA and not skip_attn:
                vtok = ("vector", b.cnt["vector"])
                b.dma("gpsimd", "dbga", scr_out.ap()[0], xn_scr[0:256, :], after=[vtok])
                b.dma("gpsimd", "dbga", scr_out.ap()[1], q_scr[0:256, :], after=[vtok])
                b.dma("gpsimd", "dbga", dbga_out.ap()[0][0:65, 0:4096], kv_sb.ap(), after=[vtok])
                b.dma("gpsimd", "dbga", dbga_out.ap()[1][:, 0:16], ugmax[:, :], after=[vtok])
                b.dma_op("gpsimd", "dbga", lambda e: e.dma_start(out=dbga_out.ap()[1][0:16, 16:17], in_=gmln[:, :], allow_slow_non_contiguous=True), after=[vtok])
                for hh in range(4):
                    b.dma("gpsimd", "dbga", dbga_out.ap()[1][0:1, 2048 + hh * 512:2048 + (hh + 1) * 512], negs_p0.ap()[0:1, hh, 0, :], after=[vtok])
                b.dma("gpsimd", "dbga", dbga_out.ap()[2][:, 0:512], qfa[:, :], after=[vtok])
                b.dma("gpsimd", "dbga", dbga_out.ap()[2][:, 512:1024], qfb[:, :], after=[vtok])
                b.dma("gpsimd", "dbga", dbga_out.ap()[3], attnbuf.ap(), after=[vtok])
                b.dma("gpsimd", "dbga", dbga_out.ap()[4][:, 0:NS * 16], negdq_tm.ap(), after=[vtok])
                b.dma("gpsimd", "dbga", dbga_out.ap()[5][:, 0:2080], kvaug.ap().rearrange("p h t n -> p (h t n)"), after=[vtok])
                tok_dbga = ("ch:dbga", b.ch_cnt["ch:dbga"])
                b._wait("sync", tok_dbga)
            if PHASES in ("krq", "krq0"):
                for chx in ("ch:oa", "ch:ob", "ch:oc"):
                    if chx in b.ch_cnt:
                        for eng in ("sync", "gpsimd", "vector"):
                            b._wait(eng, (chx, b.ch_cnt[chx]))
                qa_stack.close()
                b.finalize()
                return nc
            qa_stack.close()
            w1s = q_(nc.sbuf_tensor("w1s", [128, 2, 8, 512], BF16))
            w2s = q_(nc.sbuf_tensor("w2s", [128, 32, D], BF16))
            xe_tm = q_(nc.sbuf_tensor("xe_tm", [128, 2, D], BF16))
            xe_fm = q_(nc.sbuf_tensor("xe_fm", [128, 8, 256], BF16))
            h_buf = q_(nc.sbuf_tensor("h_buf", [128, 32, 256], BF16))
            y_stage = q_(nc.sbuf_tensor("y_stage", [128, 2, D], F32))

            tok_w1 = {}
            tok_hm_last = {}
            tok_w2 = {}
            tok_h_done = {}
            tok_y_done = {}
            tok_fm_prev = None
            tok_ysc = None
            last_relu = {}

            for ex in range(E):
                tok_w2[ex] = b.dma("gpsimd", "w2", w2s.ap(),
                                   w2_in.ap()[ex].rearrange("(c p) d -> p c d", p=128),
                                   after=[tok_y_done.get(ex - 1)] + (tok_base_list if ex == 0 else []))
                for g in range(NG):
                    gi = ex * NG + g
                    tok_gath = None
                    for j in range(2):
                        ch = ex * CPE + g * 2 + j
                        dep = ([tok_disp] + tok_base_list + tok_st_all) if gi == 0 and j == 0 else []
                        if tok_fm_prev is not None and j == 0:
                            dep = dep + [tok_fm_prev]
                        tok_gath = b.dma_op("gpsimd", "ga", lambda e, ch=ch, j=j: e.indirect_dma_start(
                            xe_tm[:, j, :], None, xn_scr[:, :],
                            IndirectOffsetOnAxis(disp_i32[:, ch:ch + 1], 0)), after=dep)
                    deps = [tok_gath, tok_h_done.get(gi - 1)]
                    tok_fm = None
                    for j in range(2):
                        for c4 in range(2):
                            for cc in range(4):
                                c = c4 * 4 + cc
                                tok_tr = b.op("tensor", lambda e, j=j, c=c, cc=cc: e.transpose(
                                    ps_smb[:, cc * 128:(cc + 1) * 128],
                                    xe_tm[:, j, c * 128:(c + 1) * 128], ident_bf[:, :]), after=deps)
                                deps = []
                            tok_fm = b.op("vector", lambda e, j=j, c4=c4: e.tensor_copy(
                                xe_fm[:, c4 * 4:(c4 + 1) * 4, j * 128:(j + 1) * 128],
                                ps_smb[:, 0:512].rearrange("p (c n) -> p c n", c=4)), after=[tok_tr])
                    tok_fm_prev = tok_fm
                    tok_h = None
                    for fs in range(8):
                        wbuf = fs % 2
                        if True:
                            dep_w = [tok_hm_last.get((gi - 1, fs + 6)) if fs < 2 else tok_hm_last.get((gi, fs - 2))
                                     ] + (tok_base_list if gi == 0 and fs < 2 else [])
                            tok_w1[(gi, fs)] = b.dma("gpsimd", "w1a" if wbuf == 0 else "w1b",
                                                     w1s[:, wbuf, :, :],
                                                     w1_in.ap()[ex].rearrange("(c p) f -> p c f", p=128)[
                                                         :, :, fs * 512:(fs + 1) * 512],
                                                     after=dep_w)
                        for fc in range(4):
                            kk = fs * 4 + fc
                            pb = 512 * (kk % 2)
                            for c in range(8):
                                first = (fc == 0 and c == 0)
                                deps = []
                                if c == 0:
                                    deps = [last_relu.get(pb)]
                                    if fc == 0:
                                        deps += [tok_fm, tok_w1[(gi, fs)]]
                                tok_hm = b.op("tensor", lambda e, wbuf=wbuf, fc=fc, c=c, pb=pb: e.matmul(
                                    ps_big[:, pb:pb + 256],
                                    w1s[:, wbuf, c, fc * 128:(fc + 1) * 128], xe_fm[:, c, 0:256],
                                    start=(c == 0), stop=(c == 7)),
                                    after=deps,
                                    auto=(gi == 0 and fs == 0 and first))
                            tok_hm_last[(gi, fs)] = tok_hm
                            tok_h = b.op("vector", lambda e, kk=kk, pb=pb: e.tensor_scalar(
                                h_buf[:, kk, :], ps_big[:, pb:pb + 256], 0.0, None, OP.max), after=[tok_hm])
                            last_relu[pb] = tok_h
                    tok_h_done[gi] = tok_h
                    for tc in range(2):
                        ch = ex * CPE + g * 2 + tc
                        ybuf = tc % 2
                        for kk in range(32):
                            for nh in range(2):
                                tok_ym = b.op("tensor", lambda e, kk=kk, tc=tc, nh=nh: e.matmul(
                                    ps_big[:, 1024 + nh * 512:1024 + (nh + 1) * 512],
                                    h_buf[:, kk, tc * 128:(tc + 1) * 128],
                                    w2s[:, kk, nh * 512:(nh + 1) * 512],
                                    start=(kk == 0), stop=(kk == 31)),
                                    after=[tok_h_done[gi], tok_w2[ex]] if (kk == 0 and nh == 0) else [])
                        tok_ye = b.op("vector", lambda e, ybuf=ybuf, ch=ch: e.tensor_scalar(
                            y_stage[:, ybuf, :], ps_big[:, 1024:2048], scale_sb[:, ch:ch + 1], None, OP.mult),
                            after=[tok_ym, tok_scale, tok_ysc])
                        tok_ysc = b.dma_op("gpsimd", "ys", lambda e, ybuf=ybuf, ch=ch: e.indirect_dma_start(
                            out_hbm[:, :], IndirectOffsetOnAxis(disp_i32[:, ch:ch + 1], 0),
                            y_stage[:, ybuf, :], None, compute_op=OP.add),
                            after=[tok_ye] + tok_base_list)
                tok_y_done[ex] = tok_ysc

            for ch in ("ch:ys", "ch:oa", "ch:ob", "ch:oc"):
                if ch in b.ch_cnt:
                    for eng in ("sync", "gpsimd", "vector"):
                        b._wait(eng, (ch, b.ch_cnt[ch]))

        b.finalize()
    return nc


def _host_inputs(N, W_qkv, W_out, proj, Wr):
    import ml_dtypes
    NS = N // 128
    ident = np.eye(128)
    d = {
        "ident_bf": ident.astype(ml_dtypes.bfloat16),
        "ident_f": ident.astype(np.float32),
        "iota8": np.broadcast_to(np.arange(8, dtype=np.float32)[None, :], (128, 8)).copy(),
        "iotacol": np.broadcast_to(np.arange(NS, dtype=np.float32)[None, :], (128, NS)).copy(),
        "onesf": np.ones((128, 128), np.float32),
        "ones1b": np.ones((1, 128), ml_dtypes.bfloat16),
    }
    hsel = np.zeros((16, 2048), np.float32)
    for h in range(16):
        hsel[h, h * 128:(h + 1) * 128] = 1.0
    d["hsel"] = hsel.astype(ml_dtypes.bfloat16)
    d["w_qkv"] = np.ascontiguousarray(W_qkv.reshape(8, 128, 3 * D))
    d["w_out"] = np.ascontiguousarray(W_out.reshape(8, 128, D))
    d["wr"] = np.ascontiguousarray(Wr.reshape(8, 128, E))
    pt = np.ascontiguousarray(proj.transpose(0, 2, 1))
    pmpair = np.zeros((128, 8, 256), np.float32)
    for c in range(8):
        pmpair[0:64, c, 0:128] = pt[2 * c]
        pmpair[64:128, c, 128:256] = pt[2 * c + 1]
    d["pmpair"] = pmpair
    pmfm = np.zeros((128, H, 128), np.float32)
    for h in range(H):
        half = 64 * (h % 2)
        pmfm[half:half + 64, h, :] = pt[h]
    d["pmfm"] = pmfm
    return d


def kernel(L0_attn_x, L0_attn_W_qkv, L0_attn_W_out, L0_attn_proj,
           L0_ffn_Wr, L0_ffn_W1, L0_ffn_W2):
    x = np.asarray(L0_attn_x, np.float32)
    N = x.shape[1]
    C_CORE = 1280
    nc = build_program(N, C_CORE, skip_attn=SKIP_ATTN)
    shared = _host_inputs(N, np.asarray(L0_attn_W_qkv, np.float32),
                          np.asarray(L0_attn_W_out, np.float32),
                          np.asarray(L0_attn_proj, np.float32),
                          np.asarray(L0_ffn_Wr, np.float32))
    shared["w1"] = np.ascontiguousarray(np.asarray(L0_ffn_W1, np.float32))
    shared["w2"] = np.ascontiguousarray(np.asarray(L0_ffn_W2, np.float32))
    in_maps = []
    for c in range(8):
        m = dict(shared)
        m["x"] = np.ascontiguousarray(x[c])
        m["rmask"] = (np.arange(8) < c).astype(np.float32)[:, None]
        in_maps.append(m)
    res = run_bass_kernel_spmd(nc, in_maps, core_ids=list(range(8)))
    return np.stack([res.results[c]["out"] for c in range(8)], axis=0)


# version 05 (final)
# speedup vs baseline: 1.0534x; 1.0005x over previous
import math
import numpy as np
from contextlib import ExitStack

import concourse.bass as bass
from concourse.bass import IndirectOffsetOnAxis
import concourse.mybir as mybir
from concourse.bass_utils import run_bass_kernel_spmd

F32 = mybir.dt.float32
BF16 = mybir.dt.bfloat16
I32 = mybir.dt.int32
AX = mybir.AxisListType.X
OP = mybir.AluOpType
AF = mybir.ActivationFunctionType

D = 1024
H = 16
E = 8
FF = 4096
LN16 = float(np.float32(math.log(16.0)))
EPS_ATTN = 1e-6
EPS_LN = 1e-5
SKIP_ATTN = False
PHASES = "all"
RACECHK = False
SERIAL = False

ENGINES = ("sync", "gpsimd", "tensor", "vector", "scalar")


class Builder:

    def __init__(self, nc, ctx):
        self.nc = nc
        self.ctx = ctx
        self.items = {k: [] for k in ENGINES}
        self.sem = {k: ctx.enter_context(nc.semaphore(f"sem_{k}")) for k in ENGINES}
        self.cnt = {k: 0 for k in ENGINES}
        self.waited = {k: {} for k in ENGINES}
        self.waitvals = {}
        self.csem = ctx.enter_context(nc.semaphore("sem_cc"))
        self.ccnt = 0
        self.ch_sem = {}
        self.ch_cnt = {}
        self.audit = []
        self.events = {k: [] for k in ENGINES}

    def _wait(self, eng, tok):
        peng, val = tok
        if peng == eng:
            return
        if peng == "ch:gw":
            val = self.ch_cnt[peng]
        if val > self.waited[eng].get(peng, 0):
            self.waited[eng][peng] = val
            self.audit.append((eng, peng, val))
            self.events[eng].append(("wait", peng, val))
            self.waitvals.setdefault(peng, set()).add(val)
            self.items[eng].append(("wait", peng, val))

    def _serial_waits(self, eng):
        for pe in ENGINES:
            if pe != eng and self.cnt[pe] > 0:
                self._wait(eng, (pe, self.cnt[pe]))
        for ch, cv in self.ch_cnt.items():
            if cv > 0:
                self._wait(eng, (ch, cv))

    def op(self, eng, fn, after=(), inc=1, auto=True):
        if SERIAL:
            self._serial_waits(eng)
        if eng == "tensor" and auto:
            self._wait(eng, ("vector", self.cnt["vector"]))
            self._wait(eng, ("scalar", self.cnt["scalar"]))
        for tok in after:
            if tok is not None:
                self._wait(eng, tok)
        self.cnt[eng] += 1
        self.events[eng].append(("inc", eng, 1))
        self.items[eng].append(("op", fn, self.cnt[eng]))
        return (eng, self.cnt[eng])

    def _dma_common(self, eng, ch, fn, after):
        ch = "ch:" + ch
        if ch not in self.ch_sem:
            self.ch_sem[ch] = self.ctx.enter_context(self.nc.semaphore(ch))
            self.ch_cnt[ch] = 0
        if SERIAL:
            self._serial_waits(eng)
        for tok in after:
            if tok is not None:
                self._wait(eng, tok)
        self.ch_cnt[ch] += 16
        self.events[eng].append(("inc", ch, 16))
        self.items[eng].append(("dma", fn, ch))
        return (ch, self.ch_cnt[ch])

    def dma(self, eng, ch, out_ap, in_ap, after=()):
        return self._dma_common(eng, ch,
                                lambda e, o=out_ap, i=in_ap: e.dma_start(out=o, in_=i), after)

    def dma_op(self, eng, ch, fn, after=()):
        return self._dma_common(eng, ch, fn, after)

    def collective(self, eng, ins_ap, outs_ap, after=()):
        for tok in after:
            if tok is not None:
                self._wait(eng, tok)
        self.ccnt += 1
        self.events[eng].append(("inc", "cc", 1))

        def run(e, i=ins_ap, o=outs_ap, cs=self.csem):
            e.collective_compute("AllGather", OP.bypass, replica_groups=[list(range(8))],
                                 ins=[i], outs=[o]).then_inc(cs, 1)
        self.items[eng].append(("raw", run))
        return ("cc", self.ccnt)

    def tok(self, eng):
        return (eng, self.cnt[eng])

    def finalize(self):
        ranks = {}
        for key, vals in self.waitvals.items():
            if key in ENGINES:
                if RACECHK:
                    vals = set(range(1, self.cnt[key] + 1))
                sv = sorted(vals)
                ranks[key] = {v: i + 1 for i, v in enumerate(sv)}
        if RACECHK:
            for key in ENGINES:
                ranks.setdefault(key, {v: v for v in range(1, self.cnt[key] + 1)})
        with self.nc.Block() as block:
            for name in ENGINES:
                items = self.items[name]
                eng_thresholds = ranks.get(name, {})
                if RACECHK:
                    eng_thresholds = {v: v for v in range(1, self.cnt[name] + 1)}
                sem = self.sem[name]

                def body(e, items=items, eng_thresholds=eng_thresholds, sem=sem, name=name):
                    for it in items:
                        kind = it[0]
                        if kind == "wait":
                            _, peng, val = it
                            if peng == "cc":
                                e.wait_ge(self.csem, val)
                            elif peng in ENGINES:
                                e.wait_ge(self.sem[peng], ranks[peng][val])
                            else:
                                e.wait_ge(self.ch_sem[peng], val)
                        elif kind == "op":
                            _, fn, cntv = it
                            if RACECHK and cntv > 1:
                                e.wait_ge(sem, cntv - 1)
                            inst = fn(e)
                            if cntv in eng_thresholds:
                                inst.then_inc(sem, 1)
                            if name == "vector":
                                e.engine_nop(); e.engine_nop(); e.engine_nop()
                            elif name in ("scalar", "gpsimd"):
                                e.nop()
                        elif kind == "dma":
                            _, fn, ch = it
                            fn(e).then_inc(self.ch_sem[ch], 16)
                        else:
                            it[1](e)
                getattr(block, name)(body)


def build_program(N, C_CORE, skip_attn=False):
    NS = N // 128
    NSLOT = E * C_CORE
    NCHUNK = NSLOT // 128
    CPE = C_CORE // 128
    PDEAD = NSLOT + NS
    PD2 = (PDEAD * 2 + 127) // 128
    CAPF = float(N)
    NG = C_CORE // 256
    assert C_CORE % 256 == 0 and N % 512 == 0

    nc = bass.Bass("TRN2", target_bir_lowering=False, debug=False, detect_race_conditions=RACECHK)
    PHASES_ = PHASES

    x_in = nc.dram_tensor("x", [N, D], F32, kind="ExternalInput")
    w_qkv = nc.dram_tensor("w_qkv", [8, 128, 3 * D], F32, kind="ExternalInput")
    w_out_in = nc.dram_tensor("w_out", [8, 128, D], F32, kind="ExternalInput")
    pmpair_in = nc.dram_tensor("pmpair", [128, 8, 256], F32, kind="ExternalInput")
    pmfm_in = nc.dram_tensor("pmfm", [128, H, 128], F32, kind="ExternalInput")
    wr_in = nc.dram_tensor("wr", [8, 128, E], F32, kind="ExternalInput")
    w1_in = nc.dram_tensor("w1", [E, D, FF], F32, kind="ExternalInput")
    w2_in = nc.dram_tensor("w2", [E, FF, D], F32, kind="ExternalInput")
    ident_bf_in = nc.dram_tensor("ident_bf", [128, 128], BF16, kind="ExternalInput")
    ident_f_in = nc.dram_tensor("ident_f", [128, 128], F32, kind="ExternalInput")
    iota8_in = nc.dram_tensor("iota8", [128, 8], F32, kind="ExternalInput")
    iotacol_in = nc.dram_tensor("iotacol", [128, NS], F32, kind="ExternalInput")
    hsel_in = nc.dram_tensor("hsel", [16, 2048], BF16, kind="ExternalInput")
    onesf_in = nc.dram_tensor("onesf", [128, 128], F32, kind="ExternalInput")
    ones1b_in = nc.dram_tensor("ones1b", [1, 128], BF16, kind="ExternalInput")
    rmask_in = nc.dram_tensor("rmask", [8, 1], F32, kind="ExternalInput")

    out_hbm = nc.dram_tensor("out", [N, D], F32, kind="ExternalOutput")
    DBG = PHASES.startswith("kr")
    if DBG:
        dbg_out = nc.dram_tensor("dbg", [6, 128, NS], F32, kind="ExternalOutput")
    DBGA = PHASES == "adbg"
    if DBGA or PHASES == "lndbg":
        dbga_out = nc.dram_tensor("dbga", [6, 128, 4096], F32, kind="ExternalOutput")
        scr_out = nc.dram_tensor("scr", [2, 256, D], BF16, kind="ExternalOutput")
        scr2_out = nc.dram_tensor("scr2", [4, 128, D], F32, kind="ExternalOutput")

    xn_scr = nc.dram_tensor("xn_scr", [N, D], BF16)
    q_scr = nc.dram_tensor("q_scr", [N, D], BF16)
    pairs_dram = nc.dram_tensor("pairs_dram", [128, PD2], F32)
    cc_in = nc.dram_tensor("cc_in", [8, 1], F32)
    cc_out = nc.dram_tensor("cc_out", [64, 1], F32)

    with ExitStack() as top:
        e_ = top.enter_context
        b = Builder(nc, top)

        pmpair = e_(nc.sbuf_tensor("pmpair_sb", [128, 8, 256], BF16))
        pmfm = e_(nc.sbuf_tensor("pmfm_sb", [128, H, 128], BF16))
        pmpair_n = e_(nc.sbuf_tensor("pmpairn_sb", [128, 8, 256], BF16))
        pmfm_n = e_(nc.sbuf_tensor("pmfmn_sb", [128, H, 128], BF16))
        ident_bf = e_(nc.sbuf_tensor("identbf_sb", [128, 128], BF16))
        ident_f = e_(nc.sbuf_tensor("identf_sb", [128, 128], F32))
        iota8 = e_(nc.sbuf_tensor("iota8_sb", [128, 8], F32))
        iotacol = e_(nc.sbuf_tensor("iotacol_sb", [128, NS], F32))
        hsel = e_(nc.sbuf_tensor("hsel_sb", [16, 2048], BF16))
        onesf = e_(nc.sbuf_tensor("onesf_sb", [128, 128], F32))
        ones1b = e_(nc.sbuf_tensor("ones1b_sb", [1, 128], BF16))
        rmask = e_(nc.sbuf_tensor("rmask_sb", [8, 1], F32))
        e_st = e_(nc.sbuf_tensor("e_st", [128, NS], F32))
        p_st = e_(nc.sbuf_tensor("p_st", [128, NS], F32))
        onehotT = e_(nc.sbuf_tensor("onehotT", [8, N], BF16))
        kv_sb = e_(nc.sbuf_tensor("kv_sb", [65, H, 256], F32))
        negdq_tm = e_(nc.sbuf_tensor("negdq_tm", [128, NS, 16], F32))
        ugmax = e_(nc.sbuf_tensor("ugmax", [128, 16], F32))
        gmln = e_(nc.sbuf_tensor("gmln", [16, 1], F32))
        disp_i32 = e_(nc.sbuf_tensor("disp_i32", [128, NCHUNK], I32))
        scale_sb = e_(nc.sbuf_tensor("scale_sb", [128, NCHUNK], F32))

        ps_big = e_(nc.psum_tensor("ps_big", [128, 2048], F32))
        ps_lg = e_(nc.psum_tensor("ps_lg", [128, 8], F32))
        ps_kvat = e_(nc.psum_tensor("ps_kvat", [65, 512], F32))
        ps_kv = ps_kvat[:, 0:256]
        ps_at = ps_kvat[:, 0:512]
        ps_sm = e_(nc.psum_tensor("ps_sm", [128, 512], F32))
        ps_smb = e_(nc.psum_tensor("ps_smb", [128, 1024], BF16))

        for dst, src in ((ident_bf[:, :], ident_bf_in[:, :]), (ident_f[:, :], ident_f_in[:, :]),
                         (iota8[:, :], iota8_in[:, :]), (iotacol[:, :], iotacol_in[:, :]),
                         (hsel[:, :], hsel_in[:, :]), (onesf[:, :], onesf_in[:, :]),
                         (ones1b[:, :], ones1b_in[:, :]), (rmask[:, :], rmask_in[:, :])):
            tok_consts = b.dma("sync", "c0", dst, src)
        tok_ug0 = b.op("vector", lambda e: e.memset(ugmax[:, :], -1e30))
        tok_kv0 = b.op("vector", lambda e: e.memset(kv_sb.ap(), 0.0))

        with ExitStack() as phk:
            k_ = phk.enter_context
            wqkv_sb = k_(nc.sbuf_tensor("wqkv_sb", [128, 8, 3 * D], BF16))
            wr_sb = k_(nc.sbuf_tensor("wr_sb", [128, 8, E], BF16))
            x_sub = k_(nc.sbuf_tensor("x_sub", [128, 2, D], F32))
            sq_bf = k_(nc.sbuf_tensor("sq_bf", [128, D], BF16))
            lncol = k_(nc.sbuf_tensor("lncol", [128, 8], F32))
            xn_tm = k_(nc.sbuf_tensor("xn_tm", [128, 2, D], BF16))
            xn_fm = k_(nc.sbuf_tensor("xn_fm", [128, 8, 128], BF16))
            q_tm = k_(nc.sbuf_tensor("q_tm", [128, 2, D], BF16))
            k_tm = k_(nc.sbuf_tensor("k_tm", [128, D], BF16))
            k_fm = k_(nc.sbuf_tensor("k_fm", [128, 8, 512], BF16))
            kfp = k_(nc.sbuf_tensor("kfp", [128, 4, H, 128], BF16))
            kfn = k_(nc.sbuf_tensor("kfn", [128, 4, H, 128], BF16))
            vbuf = k_(nc.sbuf_tensor("vbuf", [128, 4, H, 65], BF16))
            negdk_tm = k_(nc.sbuf_tensor("negdk_tm", [128, 16], F32))
            negdk_rows = k_(nc.sbuf_tensor("negdk_rows", [16, 512], BF16))
            lgf = k_(nc.sbuf_tensor("lgf", [128, 8], F32))
            lgtmp = k_(nc.sbuf_tensor("lgtmp", [128, 8], F32))
            lncol2 = k_(nc.sbuf_tensor("lncol2", [128, 4], F32))
            ohtile = k_(nc.sbuf_tensor("ohtile", [128, 8], BF16))
            lnr = k_(nc.sbuf_tensor("lnr", [128, 16], F32))
            dcol = k_(nc.sbuf_tensor("dcol", [128, 16], F32))

            tok_wq = b.dma("gpsimd", "gw", wqkv_sb.ap(), w_qkv.ap().rearrange("c p j -> p c j"))
            tok_wr = b.dma("gpsimd", "gw", wr_sb.ap(), wr_in.ap().rearrange("c p j -> p c j"))
            tok_pp0 = b.dma("gpsimd", "gw", pmpair.ap(), pmpair_in.ap())
            tok_pf0 = b.dma("gpsimd", "gw", pmfm.ap(), pmfm_in.ap())
            tok_pp = b.op("vector", lambda e: e.tensor_scalar(pmpair.ap(), pmpair.ap(), 0.125, None, OP.mult),
                          after=[tok_pf0])
            tok_pf = b.op("vector", lambda e: e.tensor_scalar(pmfm.ap(), pmfm.ap(), 0.125, None, OP.mult))
            tok_pp = b.op("vector", lambda e: e.tensor_scalar(pmpair_n.ap(), pmpair.ap(), -1.0, None, OP.mult))
            tok_pf = b.op("vector", lambda e: e.tensor_scalar(pmfm_n.ap(), pmfm.ap(), -1.0, None, OP.mult))
            tok_vb0 = b.op("vector", lambda e: e.memset(vbuf.ap(), 1.0), after=[tok_consts])

            tok_x = {0: b.dma("sync", "xa", x_sub[:, 0, :], x_in[0:128, :])}
            tok_xn = {}
            tok_xnst = {}
            tok_qst = {}
            tok_qev_d = {}
            tok_kv_done = tok_kv0

            for s in range(NS):
                si = s % 4
                buf = s % 2
                b.op("vector", lambda e, bu=buf: e.reduce_sum(lncol[:, 0:1], x_sub[:, bu, :], AX),
                     after=[tok_x[s]])
                b.op("vector", lambda e, bu=buf: e.tensor_mul(sq_bf[:, :], x_sub[:, bu, :], x_sub[:, bu, :]))
                b.op("vector", lambda e: e.reduce_sum(lncol[:, 1:2], sq_bf[:, :], AX))
                b.op("vector", lambda e: e.tensor_scalar(lncol[:, 2:3], lncol[:, 0:1], 1.0 / D, None, OP.mult))
                b.op("vector", lambda e: e.tensor_mul(lncol[:, 3:4], lncol[:, 2:3], lncol[:, 2:3]))
                b.op("vector", lambda e: e.tensor_scalar(lncol[:, 4:5], lncol[:, 1:2], 1.0 / D, EPS_LN, OP.mult, OP.add))
                tok_var = b.op("vector", lambda e: e.tensor_sub(lncol[:, 4:5], lncol[:, 4:5], lncol[:, 3:4]))
                b.op("scalar", lambda e: e.activation(lncol[:, 5:6], lncol[:, 4:5], AF.Ln),
                     after=[tok_var, tok_consts])
                tok_rstd = b.op("scalar", lambda e: e.activation(lncol[:, 6:7], lncol[:, 5:6], AF.Exp, scale=-0.5))
                tok_xn[s] = b.op("vector", lambda e, bu=buf: e.tensor_scalar(
                    xn_tm[:, bu, :], x_sub[:, bu, :], lncol[:, 2:3], lncol[:, 6:7], OP.subtract, OP.mult),
                    after=[tok_rstd, tok_xnst.get(s - 2)])
                tok_xnst[s] = b.dma("sync", "sna" if buf == 0 else "snb",
                                    xn_scr[s * 128:(s + 1) * 128, :], xn_tm[:, buf, :],
                                    after=[tok_xn[s]])
                if (DBGA or PHASES == "lndbg") and s == 0:
                    b.dma("gpsimd", "dbga", scr2_out.ap()[0], x_sub[:, 0, :], after=[tok_xn[0]])
                    b.dma("gpsimd", "dbga", scr2_out.ap()[1][:, 0:1024], sq_bf[:, :], after=[tok_xn[0]])
                    b.dma("gpsimd", "dbga", scr2_out.ap()[2][:, 0:8], lncol[:, :], after=[tok_xn[0]])
                    b.dma("gpsimd", "dbga", scr2_out.ap()[3][:, 0:1024], xn_tm[:, 0, :], after=[tok_xn[0]])
                    if PHASES == "lndbg":
                        b._wait("sync", ("ch:dbga", b.ch_cnt["ch:dbga"]))
                        b._wait("vector", ("ch:dbga", b.ch_cnt["ch:dbga"]))
                        b.finalize()
                        return nc
                if s + 1 < NS:
                    tok_x[s + 1] = b.dma("sync", "xa" if (s + 1) % 2 == 0 else "xb",
                                         x_sub[:, 1 - buf, :],
                                         x_in[(s + 1) * 128:(s + 2) * 128, :],
                                         after=[tok_xn.get(s - 1)])

                deps = [tok_xn[s], tok_consts]
                for c4 in range(2):
                    for cc in range(4):
                        c = c4 * 4 + cc
                        tok_tr = b.op("tensor", lambda e, bu=buf, c=c, cc=cc: e.transpose(
                            ps_smb[:, cc * 128:(cc + 1) * 128],
                            xn_tm[:, bu, c * 128:(c + 1) * 128], ident_bf[:, :]), after=deps)
                        deps = []
                    tok_fm = b.op("vector", lambda e, c4=c4: e.tensor_copy(
                        xn_fm[:, c4 * 4:(c4 + 1) * 4, :],
                        ps_smb[:, 0:512].rearrange("p (c n) -> p c n", c=4)), after=[tok_tr])
                tok_xnfm = tok_fm

                for c in range(8):
                    for nh in range(2):
                        tok_qmm = b.op("tensor", lambda e, c=c, nh=nh: e.matmul(
                            ps_big[:, nh * 512:(nh + 1) * 512], xn_fm[:, c, :],
                            wqkv_sb[:, c, nh * 512:(nh + 1) * 512],
                            start=(c == 0), stop=(c == 7)),
                            after=[tok_xnfm, tok_wq] if (c == 0 and nh == 0) else [])
                tok_qev = b.op("vector", lambda e, bu=buf: e.tensor_copy(q_tm[:, bu, :], ps_big[:, 0:1024]),
                               after=[tok_qmm, tok_qst.get(s - 2)])
                tok_qst[s] = b.dma("sync", "qsa" if buf == 0 else "qsb",
                                   q_scr[s * 128:(s + 1) * 128, :], q_tm[:, buf, :], after=[tok_qev])
                if not skip_attn:
                    b.op("vector", lambda e, bu=buf: e.tensor_mul(sq_bf[:, :], q_tm[:, bu, :], q_tm[:, bu, :]))
                    b.op("vector", lambda e: e.tensor_reduce(
                        dcol[:, :], sq_bf.ap().rearrange("p (h n) -> p h n", h=16), AX, OP.add))
                    b.op("vector", lambda e, s=s: e.tensor_scalar(
                        negdq_tm[:, s, :], dcol[:, :], -0.5, -LN16, OP.mult, OP.add))

                    for c in range(8):
                        for nh in range(2):
                            tok_kmm = b.op("tensor", lambda e, c=c, nh=nh: e.matmul(
                                ps_big[:, 1024 + nh * 512:1024 + (nh + 1) * 512], xn_fm[:, c, :],
                                wqkv_sb[:, c, D + nh * 512:D + (nh + 1) * 512],
                                start=(c == 0), stop=(c == 7)), after=[])
                    tok_kev = b.op("vector", lambda e: e.tensor_copy(k_tm[:, :], ps_big[:, 1024:2048]),
                                   after=[tok_kmm])
                    b.op("vector", lambda e: e.tensor_mul(sq_bf[:, :], k_tm[:, :], k_tm[:, :]))
                    b.op("vector", lambda e: e.tensor_reduce(
                        dcol[:, :], sq_bf.ap().rearrange("p (h n) -> p h n", h=16), AX, OP.add))
                    tok_ndk = b.op("vector", lambda e: e.tensor_scalar(
                        negdk_tm[:, :], dcol[:, :], -0.5, -LN16, OP.mult, OP.add))
                    tok_ndt = b.op("tensor", lambda e: e.transpose(
                        ps_sm[0:16, 0:128], negdk_tm[:, :], ident_f[:, :]), after=[tok_ndk])
                    tok_ndr = b.op("vector", lambda e, si=si: e.tensor_copy(
                        negdk_rows[:, si * 128:(si + 1) * 128], ps_sm[0:16, 0:128]), after=[tok_ndt])
                    for c4 in range(2):
                        for cc in range(4):
                            c = c4 * 4 + cc
                            tok_tr2 = b.op("tensor", lambda e, c=c, cc=cc: e.transpose(
                                ps_smb[:, cc * 128:(cc + 1) * 128],
                                k_tm[:, c * 128:(c + 1) * 128], ident_bf[:, :]), after=[])
                        tok_kfm = b.op("vector", lambda e, c4=c4, si=si: e.tensor_copy(
                            k_fm[:, c4 * 4:(c4 + 1) * 4, si * 128:(si + 1) * 128],
                            ps_smb[:, 0:512].rearrange("p (c n) -> p c n", c=4)), after=[tok_tr2])
                    tok_kfm_done = tok_kfm

                for c in range(8):
                    for nh in range(2):
                        tok_vmm = b.op("tensor", lambda e, c=c, nh=nh: e.matmul(
                            ps_big[:, nh * 512:(nh + 1) * 512], xn_fm[:, c, :],
                            wqkv_sb[:, c, 2 * D + nh * 512:2 * D + (nh + 1) * 512],
                            start=(c == 0), stop=(c == 7)), after=[tok_wr] if (c == 0 and nh == 0) else [])
                    tok_lmm = b.op("tensor", lambda e, c=c: e.matmul(
                        ps_lg[:, :], xn_fm[:, c, :], wr_sb[:, c, :],
                        start=(c == 0), stop=(c == 7)))
                if not skip_attn:
                    tok_vev = b.op("vector", lambda e, si=si: e.tensor_copy(
                        vbuf[:, si, :, 0:64],
                        ps_big[:, 0:1024].rearrange("p (h n) -> p h n", h=16)),
                        after=[tok_vmm, tok_vb0])

                b.op("vector", lambda e: e.tensor_copy(lgf[:, :], ps_lg[:, :]), after=[tok_lmm])
                b.op("vector", lambda e: e.reduce_max(lncol2[:, 3:4], lgf[:, :], AX))
                tok_nm = b.op("vector", lambda e: e.tensor_scalar(lncol2[:, 0:1], lncol2[:, 3:4], -1.0, None, OP.mult))
                tok_ex = b.op("scalar", lambda e: e.activation(lgtmp[:, :], lgf[:, :], AF.Exp, bias=lncol2[:, 0:1]),
                              after=[tok_nm])
                tok_sm = b.op("vector", lambda e: e.reduce_sum(lncol2[:, 1:2], lgtmp[:, :], AX), after=[tok_ex])
                b.op("vector", lambda e, s=s: e.reciprocal(p_st[:, s:s + 1], lncol2[:, 1:2]))
                b.op("vector", lambda e: e.tensor_single_scalar(lgtmp[:, :], lgf[:, :], lncol2[:, 3:4], OP.is_lt))
                b.op("vector", lambda e: e.scalar_tensor_tensor(lgtmp[:, :], lgtmp[:, :], 1000.0, iota8[:, :], OP.mult, OP.add))
                tok_e = b.op("vector", lambda e, s=s: e.tensor_reduce(e_st[:, s:s + 1], lgtmp[:, :], AX, OP.min))
                tok_oh = b.op("vector", lambda e, s=s: e.tensor_single_scalar(
                    ohtile[:, :], iota8[:, :], e_st[:, s:s + 1], OP.is_equal))
                tok_oht = b.op("tensor", lambda e: e.transpose(ps_smb[0:8, 0:128], ohtile[:, :], ident_bf[:, :]),
                               after=[tok_oh])
                b.op("vector", lambda e, s=s: e.tensor_copy(
                    onehotT[:, s * 128:(s + 1) * 128], ps_smb[0:8, 0:128]), after=[tok_oht])

                if not skip_attn:
                    tok_kf2 = None
                    for half in range(2):
                        hs = hsel[:, half * 1024:(half + 1) * 1024]
                        for pn in range(2):
                            for nh in range(2):
                                tok_h2 = b.op("tensor", lambda e, si=si, nh=nh, pn=pn, hs=hs: e.matmul(
                                    ps_big[:, pn * 1024 + nh * 512:pn * 1024 + (nh + 1) * 512],
                                    negdk_rows[:, si * 128:(si + 1) * 128], hs[:, nh * 512:(nh + 1) * 512],
                                    start=True, stop=False),
                                    after=[tok_vev, tok_kfm_done, tok_pp, tok_ndr, tok_kf2]
                                    if (nh == 0 and pn == 0) else [])
                        for pn, pmx in ((0, pmpair), (1, pmpair_n)):
                            for cc in range(4):
                                c = half * 4 + cc
                                tok_h2 = b.op("tensor", lambda e, c=c, cc=cc, si=si, pn=pn, pmx=pmx: e.matmul(
                                    ps_big[:, pn * 1024 + cc * 256:pn * 1024 + (cc + 1) * 256],
                                    k_fm[:, c, si * 128:(si + 1) * 128], pmx[:, c, :],
                                    start=False, stop=(cc % 2 == 1)))
                        tok_kf1 = b.op("scalar", lambda e, si=si, half=half: e.activation(
                            kfp[:, si, half * 8:(half + 1) * 8, :],
                            ps_big[:, 0:1024].rearrange("p (h n) -> p h n", h=8), AF.Exp),
                            after=[tok_h2])
                        tok_kf2 = b.op("scalar", lambda e, si=si, half=half: e.activation(
                            kfn[:, si, half * 8:(half + 1) * 8, :],
                            ps_big[:, 1024:2048].rearrange("p (h n) -> p h n", h=8), AF.Exp))
                    tok_rm = b.op("vector", lambda e, si=si: e.tensor_reduce(
                        lnr[:, :], kfp[:, si, :, :], AX, OP.max), after=[tok_kf1])
                    tok_ln = b.op("scalar", lambda e: e.activation(lnr[:, :], lnr[:, :], AF.Ln), after=[tok_rm])
                    b.op("vector", lambda e: e.tensor_sub(lnr[:, :], lnr[:, :], negdk_tm[:, :]),
                         after=[tok_ln])
                    b.op("vector", lambda e: e.tensor_max(ugmax[:, :], ugmax[:, :], lnr[:, :]),
                         after=[tok_ug0])

                    if si == 3:
                        tok_prev = None
                        for h in range(H):
                            for si2 in range(4):
                                b.op("tensor", lambda e, h=h, si2=si2: e.matmul(
                                    ps_kv[:, 0:128], vbuf[:, si2, h, :], kfp[:, si2, h, :],
                                    start=(si2 == 0), stop=False),
                                    after=[tok_kf1, tok_kf2, tok_kv_done] if (h == 0 and si2 == 0) else [])
                                tok_kvm = b.op("tensor", lambda e, h=h, si2=si2: e.matmul(
                                    ps_kv[:, 128:256], vbuf[:, si2, h, :], kfn[:, si2, h, :],
                                    start=False, stop=(si2 == 3)))
                            tok_prev = b.op("vector", lambda e, h=h: e.tensor_add(
                                kv_sb[:, h, :], kv_sb[:, h, :], ps_kv[:, :]), after=[tok_kvm])
                        tok_kv_done = tok_prev

            tok_phaseK_v = b.tok("vector")
            tok_st_all = [tok_xnst[NS - 1], tok_xnst[NS - 2], tok_qst[NS - 1], tok_qst[NS - 2]]
            tok_xnst_all = tok_xnst[NS - 1]
            tok_qst_all = tok_qst[NS - 1]
            if PHASES == "k":
                b._wait("sync", tok_phaseK_v)
                b._wait("gpsimd", tok_phaseK_v)
                b.finalize()
                return nc

            zrow8 = k_(nc.sbuf_tensor("zrow8", [8, 512], BF16))
            rkT = k_(nc.sbuf_tensor("rkT", [8, 512], F32))
            rkEx = k_(nc.sbuf_tensor("rkEx", [8, 2, 512], F32))
            carry8 = k_(nc.sbuf_tensor("carry8", [8, 1], F32))
            rk_dram = nc.dram_tensor("rk_dram", [8, N], F32)
            tok_z8 = b.op("vector", lambda e: e.memset(zrow8[:, :], 0.0))
            b.op("vector", lambda e: e.memset(carry8[:, :], 0.0))
            NCH = N // 512
            tok_spill = [None] * NCH
            for ch in range(NCH):
                tok_scan = b.op("vector", lambda e, ch=ch: e.tensor_tensor_scan(
                    rkT[:, :], onehotT[:, ch * 512:(ch + 1) * 512],
                    zrow8[:, :], 0.0, OP.add, OP.add),
                    after=[tok_phaseK_v, tok_z8] if ch == 0 else [])
                if ch > 0:
                    tok_scan = b.op("vector", lambda e, ch=ch: e.tensor_scalar_add(
                        rkT[:, :], rkT[:, :], carry8[:, 0:1]))
                if ch + 1 < NCH:
                    b.op("vector", lambda e, ch=ch: e.tensor_copy(carry8[:, :], rkT[:, 511:512]))
                par = ch % 2
                sub_deps = [tok_spill[ch - 2]] if ch >= 2 else []
                tok_sub = b.op("vector", lambda e, ch=ch, par=par: e.tensor_sub(
                    rkEx[:, par, :], rkT[:, :], onehotT[:, ch * 512:(ch + 1) * 512]),
                    after=sub_deps)
                tok_spill[ch] = b.dma("sync", "rs" + "ab"[par],
                                      rk_dram.ap()[:, ch * 512:(ch + 1) * 512],
                                      rkEx[:, par, :], after=[tok_sub])
            cnt_sb = k_(nc.sbuf_tensor("cnt_sb", [8, 1], F32))
            tok_cnt = b.op("vector", lambda e: e.tensor_copy(cnt_sb[:, :], rkT[:, 511:512]), after=[tok_scan])
            if PHASES == "kr1":
                b._wait("sync", tok_cnt)
                b._wait("gpsimd", tok_cnt)
                b.finalize()
                return nc
            tok_ccin = b.dma("gpsimd", "cb", cc_in.ap(), cnt_sb[:, :], after=[tok_cnt])
            tok_cc = b.collective("gpsimd", cc_in.ap(), cc_out.ap(), after=[tok_ccin])
            allc_sb = k_(nc.sbuf_tensor("allc_sb", [8, 8], F32))
            tok_allc = b.dma("gpsimd", "cb", allc_sb[:, :],
                             cc_out.ap().rearrange("(r e) one -> r (e one)", r=8), after=[tok_cc])
            offrow = k_(nc.sbuf_tensor("offrow", [1, 8], F32))
            offb = k_(nc.sbuf_tensor("offb", [128, 8], F32))
            tok_off = b.op("tensor", lambda e: e.matmul(ps_lg[0:1, 0:8], rmask[:, :], allc_sb[:, :],
                                                        start=True, stop=True), after=[tok_allc])
            tok_offr = b.op("vector", lambda e: e.tensor_copy(offrow[:, :], ps_lg[0:1, 0:8]), after=[tok_off])
            tok_offb_mm = b.op("tensor", lambda e: e.matmul(ps_lg[:, :], onesf[0:1, :], offrow[:, :],
                                                            start=True, stop=True), after=[tok_offr])
            tok_offb = b.op("vector", lambda e: e.tensor_copy(offb[:, :], ps_lg[:, :]), after=[tok_offb_mm])
            if PHASES == "kr2":
                b._wait("sync", tok_offb)
                b._wait("gpsimd", tok_offb)
                b.finalize()
                return nc

            lrank = k_(nc.sbuf_tensor("lrank", [128, NS], F32))
            grank = k_(nc.sbuf_tensor("grank", [128, NS], F32))
            rtmp = k_(nc.sbuf_tensor("rtmp", [128, 8], F32))
            rtmp2 = k_(nc.sbuf_tensor("rtmp2", [128, 8], F32))
            ohre = k_(nc.sbuf_tensor("ohre", [128, 8], F32))
            rkLd = k_(nc.sbuf_tensor("rkLd", [8, 2, 512], F32))
            tok_load = [None] * NCH
            tok_load[0] = b.dma("gpsimd", "rla", rkLd[:, 0, :],
                                rk_dram.ap()[:, 0:512], after=[tok_spill[0]])
            if NCH > 1:
                tok_load[1] = b.dma("gpsimd", "rlb", rkLd[:, 1, :],
                                    rk_dram.ap()[:, 512:1024], after=[tok_spill[1]])
            for s in range(NS):
                g = s // 4
                tok_rt = b.op("tensor", lambda e, s=s, g=g: e.transpose(
                    ps_sm[0:128, 0:8], rkLd[:, g % 2, (s % 4) * 128:(s % 4) * 128 + 128],
                    ident_f[0:8, 0:8]),
                    after=[tok_load[g], tok_offb] if s % 4 == 0 else [])
                if s % 4 == 3 and g + 2 < NCH:
                    tok_load[g + 2] = b.dma("gpsimd", "rl" + "ab"[g % 2],
                                            rkLd[:, g % 2, :],
                                            rk_dram.ap()[:, (g + 2) * 512:(g + 3) * 512],
                                            after=[tok_spill[g + 2], tok_rt])
                b.op("vector", lambda e, s=s: e.tensor_single_scalar(
                    ohre[:, :], iota8[:, :], e_st[:, s:s + 1], OP.is_equal), after=[tok_rt])
                b.op("vector", lambda e: e.tensor_mul(rtmp[:, :], ps_sm[0:128, 0:8], ohre[:, :]))
                b.op("vector", lambda e, s=s: e.reduce_sum(lrank[:, s:s + 1], rtmp[:, :], AX))
                b.op("vector", lambda e: e.tensor_mul(rtmp2[:, :], offb[:, :], ohre[:, :]))
                b.op("vector", lambda e, s=s: e.reduce_sum(grank[:, s:s + 1], rtmp2[:, :], AX))
            tok_gr = b.op("vector", lambda e: e.tensor_add(grank[:, :], grank[:, :], lrank[:, :]))
            keepm = k_(nc.sbuf_tensor("keepm", [128, NS], F32))
            b.op("vector", lambda e: e.tensor_single_scalar(keepm[:, :], grank[:, :], CAPF, OP.is_lt),
                 after=[tok_gr])
            tok_scl = b.op("vector", lambda e: e.tensor_mul(keepm[:, :], keepm[:, :], p_st[:, :]))
            if DBG:
                for i_, tns in enumerate((e_st, p_st, lrank, grank, keepm)):
                    b.dma("gpsimd", "dbgch", dbg_out.ap()[i_], tns[:, :], after=[tok_scl])
            if PHASES == "kr3":
                b._wait("sync", tok_scl)
                b._wait("gpsimd", ("ch:dbgch", b.ch_cnt["ch:dbgch"]))
                b.finalize()
                return nc
            ph_a = k_(nc.sbuf_tensor("ph_a", [128, NS], F32))
            ph_q = k_(nc.sbuf_tensor("ph_q", [128, NS], F32))
            ph_i = k_(nc.sbuf_tensor("ph_i", [128, NS], I32))
            ph_r = k_(nc.sbuf_tensor("ph_r", [128, NS], F32))
            b.op("vector", lambda e: e.tensor_scalar(ph_a[:, :], lrank[:, :], 1.0 / 128.0, None, OP.mult))
            b.op("vector", lambda e: e.tensor_copy(ph_i[:, :], ph_a[:, :]))
            b.op("vector", lambda e: e.tensor_copy(ph_q[:, :], ph_i[:, :]))
            b.op("vector", lambda e: e.scalar_tensor_tensor(ph_r[:, :], ph_q[:, :], -128.0, lrank[:, :], OP.mult, OP.add))
            b.op("vector", lambda e: e.tensor_single_scalar(ph_a[:, :], ph_r[:, :], 0.0, OP.is_lt))
            b.op("vector", lambda e: e.tensor_sub(ph_q[:, :], ph_q[:, :], ph_a[:, :]))
            b.op("vector", lambda e: e.scalar_tensor_tensor(ph_r[:, :], ph_a[:, :], 128.0, ph_r[:, :], OP.mult, OP.add))
            b.op("vector", lambda e: e.scalar_tensor_tensor(ph_q[:, :], e_st[:, :], float(CPE), ph_q[:, :], OP.mult, OP.add))
            b.op("vector", lambda e: e.scalar_tensor_tensor(ph_q[:, :], ph_r[:, :], float(NCHUNK), ph_q[:, :], OP.mult, OP.add))
            b.op("vector", lambda e: e.tensor_single_scalar(ph_a[:, :], lrank[:, :], float(C_CORE) - 0.5, OP.is_gt))
            b.op("vector", lambda e: e.tensor_scalar(ph_r[:, :], iotacol[:, :], 1.0, float(NSLOT), OP.mult, OP.add))
            b.op("vector", lambda e: e.tensor_sub(ph_r[:, :], ph_r[:, :], ph_q[:, :]))
            b.op("vector", lambda e: e.tensor_mul(ph_r[:, :], ph_r[:, :], ph_a[:, :]))
            b.op("vector", lambda e: e.tensor_add(ph_q[:, :], ph_q[:, :], ph_r[:, :]))
            b.op("vector", lambda e: e.tensor_scalar_max(ph_q[:, :], ph_q[:, :], 0.0))
            b.op("vector", lambda e: e.tensor_scalar_min(ph_q[:, :], ph_q[:, :], float(PDEAD - 1)))
            if DBG:
                b.dma("gpsimd", "dbgch", dbg_out.ap()[5], ph_q[:, :], after=[(
                    "vector", b.cnt["vector"])])
            tok_phi = b.op("vector", lambda e: e.tensor_copy(ph_i[:, :], ph_q[:, :]))
            prs = k_(nc.sbuf_tensor("prs", [128, NS, 2], F32))
            ids_f = k_(nc.sbuf_tensor("ids_f", [128, NS], F32))
            tok_iota = b.op("gpsimd", lambda e: e.iota(ids_f[:, :], [[128, NS]], channel_multiplier=1,
                                                       allow_small_or_imprecise_dtypes=True),
                            after=[tok_allc])
            tok_p0 = b.op("vector", lambda e: e.tensor_copy(prs[:, :, 0], ids_f[:, :]), after=[tok_iota])
            tok_p1 = b.op("vector", lambda e: e.tensor_copy(prs[:, :, 1], keepm[:, :]), after=[tok_scl])
            zpair = k_(nc.sbuf_tensor("zpair", [128, PD2], F32))
            tok_zp = b.op("vector", lambda e: e.memset(zpair[:, :], 0.0))
            tok_zpd = b.dma("sync", "pz", pairs_dram.ap(), zpair[:, :], after=[tok_zp])
            if PHASES == "kr4":
                b._wait("sync", tok_p0)
                b._wait("gpsimd", tok_zpd)
                b._wait("gpsimd", tok_p1)
                b.finalize()
                return nc
            pairs_flat = pairs_dram.ap().rearrange("p k -> (p k)")
            tok_sc_last = None
            for s in range(NS):
                deps = [tok_phi, tok_p0, tok_p1, tok_zpd] if s == 0 else []
                tok_sc_last = b.dma_op("gpsimd", "ps", lambda e, s=s: e.indirect_dma_start(
                    pairs_flat.rearrange("(r k) -> r k", k=2), IndirectOffsetOnAxis(ph_i[:, s:s + 1], 0),
                    prs[:, s, :], None), after=deps)
            prs2 = k_(nc.sbuf_tensor("prs2", [128, NCHUNK, 2], F32))
            tok_ld2 = b.dma("sync", "pl", prs2.ap(),
                            pairs_flat[0:NSLOT * 2].rearrange("(p c k) -> p c k", p=128, k=2),
                            after=[tok_sc_last])
            tok_disp = b.op("vector", lambda e: e.tensor_copy(disp_i32[:, :], prs2[:, :, 0]), after=[tok_ld2])
            tok_scale = b.op("vector", lambda e: e.tensor_copy(scale_sb[:, :], prs2[:, :, 1]))
            if PHASES == "kr5":
                b._wait("sync", tok_scale)
                b._wait("gpsimd", tok_scale)
                b.finalize()
                return nc

            if not skip_attn:
                tok_gt = b.op("tensor", lambda e: e.transpose(ps_sm[0:16, 0:128], ugmax[:, :], ident_f[:, :]),
                              after=[])
                tok_gmr = b.op("vector", lambda e: e.tensor_reduce(gmln[:, :], ps_sm[0:16, 0:128], AX, OP.max),
                               after=[tok_gt])
            else:
                tok_gmr = None
            tok_routing_g = b.tok("gpsimd")
            tok_routing_v = b.tok("vector")
            if PHASES == "kr":
                b._wait("sync", tok_scale)
                b._wait("gpsimd", tok_scale)
                b.finalize()
                return nc
            if PHASES == "kra":
                with ExitStack() as tmpq:
                    wout_t = tmpq.enter_context(nc.sbuf_tensor("wout_sb", [128, 8, D], BF16))
                    tok_wo_t = b.dma("gpsimd", "gw", wout_t.ap(), w_out_in.ap().rearrange("c p j -> p c j"),
                                     after=[tok_scale] + tok_st_all + ([tok_gmr] if tok_gmr else []))
                    b._wait("sync", tok_wo_t)
                    b._wait("vector", tok_wo_t)
                    b.finalize()
                return nc

        with ExitStack() as phq:
            q_ = phq.enter_context
            wout_sb = q_(nc.sbuf_tensor("wout_sb", [128, 8, D], BF16))
            tok_wo = b.dma("gpsimd", "gw", wout_sb.ap(), w_out_in.ap().rearrange("c p j -> p c j"), after=[tok_scale, tok_gmr] + tok_st_all)
            tok_outst = {}

            qa_stack = ExitStack()
            qa = qa_stack.enter_context
            if not skip_attn:
                kvaug = q_(nc.sbuf_tensor("kvaug_sb", [128, H, 2, 65], BF16))
                q_sub = qa(nc.sbuf_tensor("q_sub", [128, 2, D], BF16))
                q_fm = qa(nc.sbuf_tensor("q_fm", [128, 8, 512], BF16))
                x2_sub = qa(nc.sbuf_tensor("x2_sub", [128, 4, D], F32))
                stab = qa(nc.sbuf_tensor("stab", [128, 16], F32))
                negs16 = qa(nc.sbuf_tensor("negs16", [16, 128], F32))
                negs_p0 = qa(nc.sbuf_tensor("negs_p0", [1, 16, 2, 512], BF16))
                qfa = qa(nc.sbuf_tensor("qfa", [128, 512], BF16))
                qfb = qa(nc.sbuf_tensor("qfb", [128, 512], BF16))
                zr = qa(nc.sbuf_tensor("zr", [128, 512], F32))
                rb_sb = qa(nc.sbuf_tensor("rb_sb", [64, 512], F32))
                apart = qa(nc.sbuf_tensor("apart", [64, 512], BF16))
                attnbuf = qa(nc.sbuf_tensor("attnbuf", [128, 8, 512], BF16))
                out_sb = qa(nc.sbuf_tensor("out_sb", [128, 2, D], F32))

                tok_kvaug = None
                for h in range(H):
                    for j in range(2):
                        tok_tr = b.op("tensor", lambda e, h=h, j=j: e.transpose(
                            ps_sm[0:128, 0:65], kv_sb[:, h, j * 128:(j + 1) * 128], ident_f[0:65, 0:65]),
                            after=([tok_kv_done, tok_scale, tok_gmr] + tok_st_all) if (h == 0 and j == 0) else [])
                        tok_kvaug = b.op("vector", lambda e, h=h, j=j: e.tensor_copy(
                            kvaug[:, h, j, :], ps_sm[0:128, 0:65]), after=[tok_tr])

                tok_qld = {0: b.dma("sync", "qla", q_sub[:, 0, :], q_scr[0:128, :],
                                     after=[tok_scale, tok_gmr] + tok_st_all)}
                tok_xld = {}
                prev_qfm = {}
                tok_nsd = {}
                tok_oadd_last = None
                for s in range(NS):
                    t, si = s // 4, s % 4
                    buf = s % 2
                    if si == 0:
                        for si2 in range(4):
                            tok_xld[t] = b.dma("sync", "x2", x2_sub[:, si2, :],
                                               x_in[(t * 4 + si2) * 128:(t * 4 + si2 + 1) * 128, :],
                                               after=([tok_oadd_last] if t > 0 else [tok_scale, tok_gmr] + tok_st_all) if si2 == 0 else [])
                    deps = [tok_qld[s], tok_kvaug if s == 0 else None]
                    for c4 in range(2):
                        for cc in range(4):
                            c = c4 * 4 + cc
                            tok_tr = b.op("tensor", lambda e, bu=buf, c=c, cc=cc: e.transpose(
                                ps_smb[:, cc * 128:(cc + 1) * 128],
                                q_sub[:, bu, c * 128:(c + 1) * 128], ident_bf[:, :]), after=deps)
                            deps = []
                        tok_ev = b.op("vector", lambda e, c4=c4, si=si: e.tensor_copy(
                            q_fm[:, c4 * 4:(c4 + 1) * 4, si * 128:(si + 1) * 128],
                            ps_smb[:, 0:512].rearrange("p (c n) -> p c n", c=4)), after=[tok_tr])
                    prev_qfm[s] = tok_ev
                    if s + 1 < NS:
                        tok_qld[s + 1] = b.dma("sync", "qla" if (s + 1) % 2 == 0 else "qlb",
                                               q_sub[:, 1 - buf, :],
                                               q_scr[(s + 1) * 128:(s + 2) * 128, :],
                                               after=[prev_qfm.get(s - 1)])
                    for c in range(8):
                        tok_umm = b.op("tensor", lambda e, c=c, si=si: e.matmul(
                            ps_big[:, c * 256:(c + 1) * 256],
                            q_fm[:, c, si * 128:(si + 1) * 128], pmpair[:, c, :],
                            start=True, stop=True), after=[tok_ev, tok_pp] if c == 0 else [])
                    tok_stab = b.op("vector", lambda e: e.tensor_reduce(
                        stab[:, :], ps_big.ap().rearrange("p (h n) -> p h n", h=16), AX, OP.max),
                        after=[tok_umm])
                    tok_ns = b.op("vector", lambda e, s=s: e.tensor_sub(stab[:, :], negdq_tm[:, s, :], stab[:, :]),
                                  after=[tok_stab])
                    tok_nst = b.op("tensor", lambda e: e.transpose(ps_sm[0:16, 0:128], stab[:, :], ident_f[:, :]),
                                   after=[tok_ns])
                    tok_n16 = b.op("vector", lambda e: e.tensor_scalar(
                        negs16[:, :], ps_sm[0:16, 0:128], gmln[:, 0:1], None, OP.subtract),
                        after=[tok_nst, tok_gmr, tok_nsd.get(s - 1)])
                    tok_nsd[s] = b.dma("gpsimd", "nga" if t % 2 == 0 else "ngb",
                                       negs_p0.ap()[0:1, :, t % 2, si * 128:(si + 1) * 128],
                                       negs16[:, :], after=[tok_n16])

                    if si == 3:
                        tok_ab = None
                        tok_ap_prev = None
                        for h in range(H):
                            b.op("tensor", lambda e, h=h, t=t: e.matmul(
                                ps_big[:, 0:512], ones1b[:, :], negs_p0[0:1, h, t % 2, :],
                                start=True, stop=False),
                                after=[prev_qfm[s], tok_nsd[s], tok_pf] if h == 0 else [])
                            b.op("tensor", lambda e, h=h, t=t: e.matmul(
                                ps_big[:, 512:1024], ones1b[:, :], negs_p0[0:1, h, t % 2, :],
                                start=True, stop=False))
                            b.op("tensor", lambda e, h=h: e.matmul(
                                ps_big[:, 0:512], pmfm[:, h, :], q_fm[:, h // 2, :],
                                start=False, stop=True))
                            tok_bc = b.op("tensor", lambda e, h=h: e.matmul(
                                ps_big[:, 512:1024], pmfm_n[:, h, :], q_fm[:, h // 2, :],
                                start=False, stop=True))
                            tok_qf1 = b.op("scalar", lambda e: e.activation(qfa[:, :], ps_big[:, 0:512], AF.Exp),
                                           after=[tok_bc])
                            tok_qf2 = b.op("scalar", lambda e: e.activation(qfb[:, :], ps_big[:, 512:1024], AF.Exp))
                            tok_am1 = b.op("tensor", lambda e, h=h: e.matmul(
                                ps_at[:, :], kvaug[:, h, 0, :], qfa[:, :], start=True, stop=False),
                                after=[tok_qf1, tok_qf2])
                            tok_am2 = b.op("tensor", lambda e, h=h: e.matmul(
                                ps_at[:, :], kvaug[:, h, 1, :], qfb[:, :], start=False, stop=True))
                            tok_z = b.op("vector", lambda e: e.tensor_scalar(
                                zr[64:65, :], ps_at[64:65, :], 1.0, EPS_ATTN, OP.mult, OP.add),
                                after=[tok_am2])
                            tok_r = b.op("vector", lambda e: e.reciprocal(zr[64:65, :], zr[64:65, :]))
                            tok_rb = b.op("tensor", lambda e: e.matmul(
                                ps_sm[0:64, 0:512], onesf[64:65, 0:64], zr[64:65, :], start=True, stop=True),
                                after=[tok_r])
                            tok_rbe = b.op("vector", lambda e: e.tensor_copy(rb_sb[:, :], ps_sm[0:64, 0:512]),
                                           after=[tok_rb])
                            tok_ap = b.op("vector", lambda e: e.scalar_tensor_tensor(
                                apart[:, :], ps_at[0:64, :], 1.0, rb_sb[:, :], OP.mult, OP.mult),
                                after=[tok_rbe])
                            tok_ap_prev = tok_ap
                            if h % 2 == 0:
                                tok_ab = b.op("vector", lambda e, h=h: e.tensor_copy(
                                    attnbuf[0:64, h // 2, :], apart[:, :]), after=[tok_ap])
                            else:
                                tok_sh = b.op("tensor", lambda e: e.matmul(
                                    ps_big[64:128, 1024:1536], ident_bf[0:64, 0:64], apart[:, :],
                                    start=True, stop=True), after=[tok_ap])
                                tok_ab = b.op("vector", lambda e, h=h: e.tensor_copy(
                                    attnbuf[64:128, h // 2, :], ps_big[64:128, 1024:1536]), after=[tok_sh])
                        for si2 in range(4):
                            s2 = t * 4 + si2
                            bu2 = s2 % 2
                            for c in range(8):
                                for nh in range(2):
                                    tok_wm = b.op("tensor", lambda e, c=c, si2=si2, nh=nh: e.matmul(
                                        ps_big[:, nh * 512:(nh + 1) * 512],
                                        attnbuf[:, c, si2 * 128:(si2 + 1) * 128],
                                        wout_sb[:, c, nh * 512:(nh + 1) * 512],
                                        start=(c == 0), stop=(c == 7)),
                                        after=[tok_ab, tok_wo] if (c == 0 and nh == 0) else [])
                            tok_oadd = b.op("vector", lambda e, bu2=bu2, si2=si2: e.tensor_add(
                                out_sb[:, bu2, :], x2_sub[:, si2, :], ps_big[:, 0:1024]),
                                after=[tok_wm, tok_xld[t], tok_outst.get(s2 - 2)])
                            tok_outst[s2] = b.dma("sync", "oa" if bu2 == 0 else "ob",
                                                  out_hbm[s2 * 128:(s2 + 1) * 128, :],
                                                  out_sb[:, bu2, :], after=[tok_oadd])
                            tok_oadd_last = tok_oadd
                tok_base_done = tok_outst[NS - 1]
                tok_base_list = [tok_outst[NS - 1], tok_outst[NS - 2]]
            else:
                x2_sub = qa(nc.sbuf_tensor("x2_sub", [128, 2, D], F32))
                tok_xld = {0: b.dma("sync", "x2a", x2_sub[:, 0, :], x_in[0:128, :], after=[tok_scale] + tok_st_all + ([tok_gmr] if tok_gmr else []))}
                tok_cp = {}
                for s in range(NS):
                    buf = s % 2
                    if s + 1 < NS:
                        tok_xld[s + 1] = b.dma("sync", "x2a" if (s + 1) % 2 == 0 else "x2b",
                                               x_sub[:, 1 - buf, :] if False else x2_sub[:, 1 - buf, :],
                                               x_in[(s + 1) * 128:(s + 2) * 128, :],
                                               after=[tok_cp.get(s - 1)])
                    if PHASES == "krq0":
                        tok_cp[s] = b.dma("gpsimd", "oc", out_hbm[0:128, :], x_in[0:128, :],
                                          after=[tok_xld[s]])
                    else:
                        tok_cp[s] = b.dma("gpsimd", "oc", out_hbm[s * 128:(s + 1) * 128, :], x2_sub[:, buf, :],
                                          after=[tok_xld[s]])
                tok_base_done = tok_cp[NS - 1]
                tok_base_list = [tok_base_done]

            if DBGA and not skip_attn:
                vtok = ("vector", b.cnt["vector"])
                b.dma("gpsimd", "dbga", scr_out.ap()[0], xn_scr[0:256, :], after=[vtok])
                b.dma("gpsimd", "dbga", scr_out.ap()[1], q_scr[0:256, :], after=[vtok])
                b.dma("gpsimd", "dbga", dbga_out.ap()[0][0:65, 0:4096], kv_sb.ap(), after=[vtok])
                b.dma("gpsimd", "dbga", dbga_out.ap()[1][:, 0:16], ugmax[:, :], after=[vtok])
                b.dma_op("gpsimd", "dbga", lambda e: e.dma_start(out=dbga_out.ap()[1][0:16, 16:17], in_=gmln[:, :], allow_slow_non_contiguous=True), after=[vtok])
                for hh in range(4):
                    b.dma("gpsimd", "dbga", dbga_out.ap()[1][0:1, 2048 + hh * 512:2048 + (hh + 1) * 512], negs_p0.ap()[0:1, hh, 0, :], after=[vtok])
                b.dma("gpsimd", "dbga", dbga_out.ap()[2][:, 0:512], qfa[:, :], after=[vtok])
                b.dma("gpsimd", "dbga", dbga_out.ap()[2][:, 512:1024], qfb[:, :], after=[vtok])
                b.dma("gpsimd", "dbga", dbga_out.ap()[3], attnbuf.ap(), after=[vtok])
                b.dma("gpsimd", "dbga", dbga_out.ap()[4][:, 0:NS * 16], negdq_tm.ap(), after=[vtok])
                b.dma("gpsimd", "dbga", dbga_out.ap()[5][:, 0:2080], kvaug.ap().rearrange("p h t n -> p (h t n)"), after=[vtok])
                tok_dbga = ("ch:dbga", b.ch_cnt["ch:dbga"])
                b._wait("sync", tok_dbga)
            if PHASES in ("krq", "krq0"):
                for chx in ("ch:oa", "ch:ob", "ch:oc"):
                    if chx in b.ch_cnt:
                        for eng in ("sync", "gpsimd", "vector"):
                            b._wait(eng, (chx, b.ch_cnt[chx]))
                qa_stack.close()
                b.finalize()
                return nc
            qa_stack.close()
            w1s = q_(nc.sbuf_tensor("w1s", [128, 2, 8, 512], BF16))
            w2s = q_(nc.sbuf_tensor("w2s", [128, 32, D], BF16))
            xe_tm = q_(nc.sbuf_tensor("xe_tm", [128, 2, D], BF16))
            xe_fm = q_(nc.sbuf_tensor("xe_fm", [128, 8, 256], BF16))
            h_buf = q_(nc.sbuf_tensor("h_buf", [128, 32, 256], BF16))
            y_stage = q_(nc.sbuf_tensor("y_stage", [128, 2, D], F32))

            tok_w1 = {}
            tok_hm_last = {}
            tok_w2 = {}
            tok_h_done = {}
            tok_y_done = {}
            tok_fm_prev = None
            tok_ysc = None

            for ex in range(E):
                tok_w2[ex] = b.dma("gpsimd", "w2", w2s.ap(),
                                   w2_in.ap()[ex].rearrange("(c p) d -> p c d", p=128),
                                   after=[tok_y_done.get(ex - 1)] + (tok_base_list if ex == 0 else []))
                for g in range(NG):
                    gi = ex * NG + g
                    tok_gath = None
                    for j in range(2):
                        ch = ex * CPE + g * 2 + j
                        dep = ([tok_disp] + tok_base_list + tok_st_all) if gi == 0 and j == 0 else []
                        if tok_fm_prev is not None and j == 0:
                            dep = dep + [tok_fm_prev]
                        tok_gath = b.dma_op("gpsimd", "ga", lambda e, ch=ch, j=j: e.indirect_dma_start(
                            xe_tm[:, j, :], None, xn_scr[:, :],
                            IndirectOffsetOnAxis(disp_i32[:, ch:ch + 1], 0)), after=dep)
                    deps = [tok_gath, tok_h_done.get(gi - 1)]
                    tok_fm = None
                    for j in range(2):
                        for c4 in range(2):
                            for cc in range(4):
                                c = c4 * 4 + cc
                                tok_tr = b.op("tensor", lambda e, j=j, c=c, cc=cc: e.transpose(
                                    ps_smb[:, cc * 128:(cc + 1) * 128],
                                    xe_tm[:, j, c * 128:(c + 1) * 128], ident_bf[:, :]), after=deps)
                                deps = []
                            tok_fm = b.op("vector", lambda e, j=j, c4=c4: e.tensor_copy(
                                xe_fm[:, c4 * 4:(c4 + 1) * 4, j * 128:(j + 1) * 128],
                                ps_smb[:, 0:512].rearrange("p (c n) -> p c n", c=4)), after=[tok_tr])
                    tok_fm_prev = tok_fm
                    tok_h = None
                    for fs in range(8):
                        wbuf = fs % 2
                        if True:
                            dep_w = [tok_hm_last.get((gi - 1, fs + 6)) if fs < 2 else tok_hm_last.get((gi, fs - 2))
                                     ] + (tok_base_list if gi == 0 and fs < 2 else [])
                            tok_w1[(gi, fs)] = b.dma("gpsimd", "w1a" if wbuf == 0 else "w1b",
                                                     w1s[:, wbuf, :, :],
                                                     w1_in.ap()[ex].rearrange("(c p) f -> p c f", p=128)[
                                                         :, :, fs * 512:(fs + 1) * 512],
                                                     after=dep_w)
                        tok_hm_fc = {}
                        for fc in range(4):
                            pb = 512 * fc
                            for c in range(8):
                                tok_hm = b.op("tensor", lambda e, wbuf=wbuf, fc=fc, c=c, pb=pb: e.matmul(
                                    ps_big[:, pb:pb + 256],
                                    w1s[:, wbuf, c, fc * 128:(fc + 1) * 128], xe_fm[:, c, 0:256],
                                    start=(c == 0), stop=(c == 7)),
                                    after=[tok_fm, tok_w1[(gi, fs)]] if (fc == 0 and c == 0) else [])
                            tok_hm_fc[fc] = tok_hm
                        tok_hm_last[(gi, fs)] = tok_hm
                        for fc in range(4):
                            kk = fs * 4 + fc
                            tok_h = b.op("vector", lambda e, kk=kk, fc=fc: e.tensor_scalar(
                                h_buf[:, kk, :], ps_big[:, 512 * fc:512 * fc + 256], 0.0, None, OP.max),
                                after=[tok_hm_fc[fc]])
                    tok_h_done[gi] = tok_h
                    for tc in range(2):
                        ch = ex * CPE + g * 2 + tc
                        ybuf = tc % 2
                        for kk in range(32):
                            for nh in range(2):
                                tok_ym = b.op("tensor", lambda e, kk=kk, tc=tc, nh=nh: e.matmul(
                                    ps_big[:, 1024 + nh * 512:1024 + (nh + 1) * 512],
                                    h_buf[:, kk, tc * 128:(tc + 1) * 128],
                                    w2s[:, kk, nh * 512:(nh + 1) * 512],
                                    start=(kk == 0), stop=(kk == 31)),
                                    after=[tok_h_done[gi], tok_w2[ex]] if (kk == 0 and nh == 0) else [])
                        tok_ye = b.op("vector", lambda e, ybuf=ybuf, ch=ch: e.tensor_scalar(
                            y_stage[:, ybuf, :], ps_big[:, 1024:2048], scale_sb[:, ch:ch + 1], None, OP.mult),
                            after=[tok_ym, tok_scale, tok_ysc])
                        tok_ysc = b.dma_op("gpsimd", "ys", lambda e, ybuf=ybuf, ch=ch: e.indirect_dma_start(
                            out_hbm[:, :], IndirectOffsetOnAxis(disp_i32[:, ch:ch + 1], 0),
                            y_stage[:, ybuf, :], None, compute_op=OP.add),
                            after=[tok_ye] + tok_base_list)
                tok_y_done[ex] = tok_ysc

            for ch in ("ch:ys", "ch:oa", "ch:ob", "ch:oc"):
                if ch in b.ch_cnt:
                    for eng in ("sync", "gpsimd", "vector"):
                        b._wait(eng, (ch, b.ch_cnt[ch]))

        b.finalize()
    return nc


def _host_inputs(N, W_qkv, W_out, proj, Wr):
    import ml_dtypes
    NS = N // 128
    ident = np.eye(128)
    d = {
        "ident_bf": ident.astype(ml_dtypes.bfloat16),
        "ident_f": ident.astype(np.float32),
        "iota8": np.broadcast_to(np.arange(8, dtype=np.float32)[None, :], (128, 8)).copy(),
        "iotacol": np.broadcast_to(np.arange(NS, dtype=np.float32)[None, :], (128, NS)).copy(),
        "onesf": np.ones((128, 128), np.float32),
        "ones1b": np.ones((1, 128), ml_dtypes.bfloat16),
    }
    hsel = np.zeros((16, 2048), np.float32)
    for h in range(16):
        hsel[h, h * 128:(h + 1) * 128] = 1.0
    d["hsel"] = hsel.astype(ml_dtypes.bfloat16)
    d["w_qkv"] = np.ascontiguousarray(W_qkv.reshape(8, 128, 3 * D))
    d["w_out"] = np.ascontiguousarray(W_out.reshape(8, 128, D))
    d["wr"] = np.ascontiguousarray(Wr.reshape(8, 128, E))
    pt = np.ascontiguousarray(proj.transpose(0, 2, 1))
    pmpair = np.zeros((128, 8, 256), np.float32)
    for c in range(8):
        pmpair[0:64, c, 0:128] = pt[2 * c]
        pmpair[64:128, c, 128:256] = pt[2 * c + 1]
    d["pmpair"] = pmpair
    pmfm = np.zeros((128, H, 128), np.float32)
    for h in range(H):
        half = 64 * (h % 2)
        pmfm[half:half + 64, h, :] = pt[h]
    d["pmfm"] = pmfm
    return d


def kernel(L0_attn_x, L0_attn_W_qkv, L0_attn_W_out, L0_attn_proj,
           L0_ffn_Wr, L0_ffn_W1, L0_ffn_W2):
    x = np.asarray(L0_attn_x, np.float32)
    N = x.shape[1]
    C_CORE = 1280
    nc = build_program(N, C_CORE, skip_attn=SKIP_ATTN)
    shared = _host_inputs(N, np.asarray(L0_attn_W_qkv, np.float32),
                          np.asarray(L0_attn_W_out, np.float32),
                          np.asarray(L0_attn_proj, np.float32),
                          np.asarray(L0_ffn_Wr, np.float32))
    shared["w1"] = np.ascontiguousarray(np.asarray(L0_ffn_W1, np.float32))
    shared["w2"] = np.ascontiguousarray(np.asarray(L0_ffn_W2, np.float32))
    in_maps = []
    for c in range(8):
        m = dict(shared)
        m["x"] = np.ascontiguousarray(x[c])
        m["rmask"] = (np.arange(8) < c).astype(np.float32)[:, None]
        in_maps.append(m)
    res = run_bass_kernel_spmd(nc, in_maps, core_ids=list(range(8)))
    return np.stack([res.results[c]["out"] for c in range(8)], axis=0)
